# version 04
# baseline (speedup 1.0000x reference)
import contextlib

import numpy as np
import ml_dtypes
import concourse.bass as bass
import concourse.mybir as mybir
from concourse.bass_utils import run_bass_kernel_spmd

F32 = mybir.dt.float32
BF16 = mybir.dt.bfloat16
I8 = mybir.dt.int8
AF = mybir.ActivationFunctionType
ALU = mybir.AluOpType
AX = mybir.AxisListType
BF = ml_dtypes.bfloat16

D = 4096
CT = 32
SEQ = 2048
TOK = 1024
DFF = 11008
FT = 86
W = 512
HD = 1024
DT = 8
EPS_LN = 1e-5
EPS_B = 1e-6
SM_SCALE = 1.0 / 32.0
FQ = [(0, 22), (22, 21), (43, 22), (65, 21)]

S1_GROUPS = 4
S1_SG = 2
S1_OUTPROJ = True
S1_PASSES = ("k", "v", "q")
S1_SGP = ("s", "t", "a")


class Counts:
    def __init__(self):
        self.c = {}

    def inc(self, inst, sem, name, n=1):
        inst.then_inc(sem, n)
        self.c[name] = self.c.get(name, 0) + n
        return self.c[name]

    def get(self, name):
        return self.c.get(name, 0)


class _Stop(Exception):
    pass


def build_program(upto=4, trunc=None):
    nc = bass.Bass("TRN2", target_bir_lowering=False)
    global _last_nc
    _last_nc = nc

    def din(name, shape, dtype):
        return nc.dram_tensor(name, shape, dtype, kind="ExternalInput").ap()

    xT = din("xT", [D, SEQ], F32)
    wq_a = din("wq_a", [D, HD], F32)
    wk_a = din("wk_a", [D, HD], F32)
    wv_a = din("wv_a", [D, HD], F32)
    bqk_t = din("bqk_t", [128, 16], F32)
    bv_row = din("bv_row", [1, HD], F32)
    wo_a = din("wo_a", [CT * 128, HD], F32)
    bo_t = din("bo_t", [128, CT], F32)
    gate_t = din("gate_t", [FT * 128, D], I8)
    up_t = din("up_t", [FT * 128, D], I8)
    down_t = din("down_t", [CT * 128, DFF], I8)
    gs_t = din("gs_t", [128, FT], F32)
    us_t = din("us_t", [128, FT], F32)
    ds_t = din("ds_t", [128, CT], F32)
    wq_b = din("wq_b", [D, HD], F32)
    bq_b = din("bq_b", [128, 8], F32)
    wo_b = din("wo_b", [CT * 128, D], F32)
    m_t = din("m_t", [128, 8 * 128], F32)
    z_t = din("z_t", [128, 8], F32)
    cvals = din("cvals", [128, 4], F32)
    ones_col = din("ones_col", [128, 1], BF16)
    ones_row = din("ones_row", [1, 128], BF16)
    ident = din("ident", [128, 128], BF16)
    outT = nc.dram_tensor("outT", [D, TOK], F32, kind="ExternalOutput").ap()

    C = Counts()

    with contextlib.ExitStack() as st:
        _uid = [0]

        def _mang(name):
            _uid[0] += 1
            return f"{name}_{_uid[0]}"

        def sbuf_t(name, shape, dtype):
            return nc.sbuf_tensor(_mang(name), shape, dtype)

        def psum_t(name, shape, dtype):
            return nc.psum_tensor(_mang(name), shape, dtype)

        def sb(name, shape, dtype):
            return st.enter_context(nc.sbuf_tensor(name, shape, dtype))

        c_bqk = sb("c_bqk", [128, 16], F32)
        c_bo = sb("c_bo", [128, CT], F32)
        c_gs = sb("c_gs", [128, FT], F32)
        c_us = sb("c_us", [128, FT], F32)
        c_ds = sb("c_ds", [128, CT], F32)
        c_bqb = sb("c_bqb", [128, 8], F32)
        c_ones_c = sb("c_ones_c", [128, 1], BF16)
        c_ones_r = sb("c_ones_r", [1, 128], BF16)
        c_id = sb("c_id", [128, 128], BF16)
        c_bv = sb("c_bv", [128, HD], BF16)
        c_m = sb("c_m", [128, 8 * 128], BF16)
        c_z = sb("c_z", [128, 8], BF16)
        sq = sb("sq", [128, 2 * W], BF16)
        ln_rows = sb("ln_rows", [1, 4 * W], F32)
        ln_rbf = sb("ln_rbf", [1, 2 * W], BF16)
        m_b = sb("m_b", [128, W], BF16)
        r_b = sb("r_b", [128, W], BF16)
        t_ln = sb("t_ln", [128, 2 * W], BF16)
        t_eA = sb("t_eA", [128, 2 * W], BF16)
        t_eB = sb("t_eB", [128, 2 * W], BF16)
        t_eC = sb("t_eC", [128, 2 * W], BF16)
        t_y = sb("t_y", [128, 2 * TOK], BF16)
        t_of = sb("t_of", [128, 2 * W], F32)
        c_cv = sb("c_cv", [128, 4], F32)
        for _i, _v in enumerate((EPS_LN, EPS_B, SM_SCALE, 1.0 / D)):
            nc.const_aps.register(F32, _v, c_cv[:, _i:_i + 1])
        s_w = st.enter_context(nc.semaphore("s_w"))
        s_wA = [st.enter_context(nc.semaphore("s_wA0")),
                st.enter_context(nc.semaphore("s_wA1"))]
        s_wB = [st.enter_context(nc.semaphore("s_wB0")),
                st.enter_context(nc.semaphore("s_wB1"))]
        s_wD = [st.enter_context(nc.semaphore("s_wD0")),
                st.enter_context(nc.semaphore("s_wD1"))]
        s_o = [st.enter_context(nc.semaphore("s_o0")),
               st.enter_context(nc.semaphore("s_o1"))]
        NA = ("s_wA0", "s_wA1")
        NB = ("s_wB0", "s_wB1")
        ND = ("s_wD0", "s_wD1")
        NO = ("s_o0", "s_o1")
        s_c = st.enter_context(nc.semaphore("s_c"))
        s_pe = st.enter_context(nc.semaphore("s_pe"))
        s_act = st.enter_context(nc.semaphore("s_act"))
        s_dve = st.enter_context(nc.semaphore("s_dve"))
        _blk_holder = [nc.Block().__enter__()]
        st.callback(lambda: _blk_holder[0].__exit__(None, None, None))

        class _BlkProxy:
            def tensor(self, f):
                return _blk_holder[0].tensor(f)

            def scalar(self, f):
                return _blk_holder[0].scalar(f)

            def vector(self, f):
                return _blk_holder[0].vector(f)

            def gpsimd(self, f):
                return _blk_holder[0].gpsimd(f)

            def sync(self, f):
                return _blk_holder[0].sync(f)

        blk = _BlkProxy()
        buf = {"resid": None}

        def resid():
            if buf["resid"] is None:
                buf["resid"] = st.enter_context(
                    nc.sbuf_tensor("resid", [128, CT * TOK], BF16, side="right"))
            return buf["resid"]

        _barn = [0]

        def bar():
            _blk_holder[0].__exit__(None, None, None)
            _blk_holder[0] = nc.Block().__enter__()
            _barn[0] += 1
            if trunc is not None and _barn[0] >= trunc:
                raise _Stop()
            tgt = [(s_pe, C.get("s_pe")), (s_act, C.get("s_act")),
                   (s_dve, C.get("s_dve"))]
            for deco in (blk.tensor, blk.scalar, blk.vector, blk.gpsimd,
                         blk.sync):
                @deco
                def _(e):
                    for se, val in tgt:
                        if val:
                            e.wait_ge(se, val)

        @blk.sync
        def _(sy):
            for dst, src in (
                (c_bqk, bqk_t), (c_bo, bo_t), (c_gs, gs_t), (c_us, us_t),
                (c_ds, ds_t), (c_bqb, bq_b), (c_cv, cvals), (c_ones_c, ones_col),
                (c_ones_r, ones_row), (c_id, ident),
            ):
                C.inc(sy.dma_start(dst[:], src[:]), s_c, "s_c", 16)

        @blk.gpsimd
        def _(g):
            C.inc(g.dma_start(c_m[:], m_t[:]), s_w, "s_w", 16)
            C.inc(g.dma_start(c_z[:], z_t[:]), s_w, "s_w", 16)

        with sbuf_t("bv_r", [1, HD], F32) as bv_r, \
             sbuf_t("bv_rb", [1, HD], BF16) as bv_rb, \
             psum_t("pb0", [128, HD], F32) as pb0:
            @blk.sync
            def _(sy):
                C.inc(sy.dma_start(bv_r[:], bv_row[:]), s_c, "s_c", 16)

            @blk.vector
            def _(v):
                v.wait_ge(s_c, C.get("s_c"))
                C.inc(v.tensor_copy(bv_rb[:], bv_r[:]), s_dve, "s_dve")

            @blk.tensor
            def _(t):
                t.wait_ge(s_dve, C.get("s_dve"))
                t.wait_ge(s_w, C.get("s_w"))
                for h in range(2):
                    last = t.matmul(pb0[:, h * W:(h + 1) * W], c_ones_r[:],
                                    bv_rb[:, h * W:(h + 1) * W], start=True, stop=True)
                C.inc(last, s_pe, "s_pe")

            @blk.scalar
            def _(sc):
                sc.wait_ge(s_pe, C.get("s_pe"))
                C.inc(sc.copy(c_bv[:], pb0[:]), s_act, "s_act")
            for deco in (blk.tensor, blk.scalar, blk.vector, blk.gpsimd,
                         blk.sync):
                @deco
                def _(e):
                    e.wait_ge(s_c, C.get("s_c"))
                    e.wait_ge(s_w, C.get("s_w"))
            bar()

        def emit_ln(src_of_ct, do_bar=True, pe_wait=None, dve_wait=None):
            with psum_t("ps_sum", [1, W], F32) as ps_sum, \
                 psum_t("ps_sq", [1, W], F32) as ps_sq, \
                 psum_t("ps_bc", [128, 2 * W], F32) as ps_bc:
                a0 = C.get("s_act")
                p0 = C.get("s_pe")

                @blk.scalar
                def _(sc):
                    if pe_wait:
                        sc.wait_ge(s_pe, pe_wait)
                    if dve_wait:
                        sc.wait_ge(s_dve, dve_wait)
                    for ct in range(CT):
                        if ct >= 2:
                            sc.wait_ge(s_pe, p0 + ct - 1)
                        C.inc(sc.activation(sq[:, (ct % 2) * W:(ct % 2 + 1) * W],
                                            src_of_ct(ct), AF.Square), s_act, "s_act")

                @blk.tensor
                def _(t):
                    if pe_wait is None and dve_wait:
                        t.wait_ge(s_dve, dve_wait)
                    for ct in range(CT):
                        t.wait_ge(s_act, a0 + ct + 1)
                        t.matmul(ps_sum[:], c_ones_c[:], src_of_ct(ct),
                                 start=(ct == 0), stop=(ct == CT - 1))
                        C.inc(t.matmul(ps_sq[:], c_ones_c[:],
                                       sq[:, (ct % 2) * W:(ct % 2 + 1) * W],
                                       start=(ct == 0), stop=(ct == CT - 1)),
                              s_pe, "s_pe")

                m_row = ln_rows[:, 0:W]
                msq_row = ln_rows[:, W:2 * W]
                var_row = ln_rows[:, 2 * W:3 * W]
                sd_row = ln_rows[:, 3 * W:4 * W]
                m_rb = ln_rbf[:, 0:W]
                rs_rb = ln_rbf[:, W:2 * W]
                pe_stats = C.get("s_pe")

                @blk.vector
                def _(v):
                    v.wait_ge(s_pe, pe_stats)
                    C.inc(v.tensor_scalar_mul(m_row, ps_sum[:], 1.0 / D),
                          s_dve, "s_dve")

                @blk.scalar
                def _(sc):
                    sc.wait_ge(s_pe, pe_stats)
                    C.inc(sc.activation(msq_row, ps_sum[:], AF.Square, scale=1.0 / D),
                          s_act, "s_act")
                a_msq = C.get("s_act")

                @blk.vector
                def _(v):
                    v.wait_ge(s_act, a_msq)
                    C.inc(v.scalar_tensor_tensor(var_row, ps_sq[:], 1.0 / D, msq_row,
                                                 ALU.mult, ALU.subtract),
                          s_dve, "s_dve")
                d_var = C.get("s_dve")

                @blk.scalar
                def _(sc):
                    sc.wait_ge(s_dve, d_var)
                    C.inc(sc.activation(sd_row, var_row, AF.Sqrt, bias=EPS_LN),
                          s_act, "s_act")
                a_sd = C.get("s_act")

                @blk.vector
                def _(v):
                    v.wait_ge(s_act, a_sd)
                    rc = C.inc(v.reciprocal(sd_row, sd_row), s_dve, "s_dve")
                    v.tensor_copy(m_rb, m_row)
                    v.wait_ge(s_dve, rc)
                    C.inc(v.tensor_copy(rs_rb, sd_row), s_dve, "s_dve")
                d_rows = C.get("s_dve")

                @blk.tensor
                def _(t):
                    t.wait_ge(s_dve, d_rows)
                    t.matmul(ps_bc[:, 0:W], c_ones_r[:], m_rb, start=True, stop=True)
                    C.inc(t.matmul(ps_bc[:, W:2 * W], c_ones_r[:], rs_rb,
                                   start=True, stop=True), s_pe, "s_pe")
                pe_bc = C.get("s_pe")

                @blk.scalar
                def _(sc):
                    sc.wait_ge(s_pe, pe_bc)
                    sc.copy(m_b[:], ps_bc[:, 0:W])
                    C.inc(sc.copy(r_b[:], ps_bc[:, W:2 * W]), s_act, "s_act")
                if do_bar:
                    bar()
                return C.get("s_act")

        def emit_ln_apply(src_of_ct, dst_of_ct, do_bar=True, act_wait=None,
                          pe_wait=None):
            @blk.vector
            def _(v):
                if act_wait:
                    v.wait_ge(s_act, act_wait)
                if pe_wait:
                    v.wait_ge(s_pe, pe_wait)
                for ct in range(CT):
                    tv = t_ln[:, (ct % 2) * W:(ct % 2 + 1) * W]
                    sc_ = C.inc(v.tensor_sub(tv, src_of_ct(ct), m_b[:]),
                                s_dve, "s_dve")
                    v.wait_ge(s_dve, sc_)
                    C.inc(v.tensor_mul(dst_of_ct(ct), tv, r_b[:]), s_dve, "s_dve")
            if do_bar:
                bar()
            return C.get("s_dve")

        def emit_s1():
            with sbuf_t("ws1", [128, 2 * HD], BF16) as ws1, \
                 sbuf_t("ao_sb", [128, DT * TOK], BF16) as ao_sb, \
                 contextlib.ExitStack() as st_s1:
                k_sb = st_s1.enter_context(sbuf_t("k_sb", [128, DT * SEQ], BF16))
                v_sb = st_s1.enter_context(sbuf_t("v_sb", [128, 16 * HD], BF16))
                q_sb = st_s1.enter_context(sbuf_t("q_sb", [128, DT * TOK], BF16))
                st_xg = contextlib.ExitStack()
                xg = st_xg.enter_context(sbuf_t("xg", [128, CT * W], BF16))
                pass_done = None
                evac_done = None
                for grp in range(S1_GROUPS):
                    @blk.gpsimd
                    def _(g, pd=pass_done):
                        g.wait_ge(s_w, C.get("s_w"))
                        if pd:
                            g.wait_ge(s_pe, pd)
                        C.inc(g.dma_start(
                            xg[:].rearrange("p (ct t) -> p ct t", ct=CT),
                            xT[:, grp * W:(grp + 1) * W]
                            .rearrange("(ct p) t -> p ct t", p=128)),
                            s_w, "s_w", 16)
                    wv_xg = C.get("s_w")
                    for deco in (blk.scalar, blk.tensor, blk.vector):
                        @deco
                        def _(e):
                            e.wait_ge(s_w, wv_xg)
                    a_bc = emit_ln(lambda ct: xg[:, ct * W:(ct + 1) * W],
                                   do_bar=False, pe_wait=pass_done,
                                   dve_wait=evac_done)
                    apply_done = emit_ln_apply(
                        lambda ct: xg[:, ct * W:(ct + 1) * W],
                        lambda ct: xg[:, ct * W:(ct + 1) * W],
                        do_bar=False, act_wait=a_bc, pe_wait=pass_done)

                    def proj_pass(wmat, kind, grp=grp):
                        nonlocal pass_done, evac_done
                        with psum_t("ps_p", [128, 8 * W], F32) as ps_p:
                            wb = [C.get(NA[0]), C.get(NA[1])]
                            pe0 = C.get("s_pe")
                            pd_prev = pass_done
                            ev_prev = evac_done
                            ap_d = apply_done

                            @blk.gpsimd
                            def _(g):
                                for ct in range(CT):
                                    if ct >= 2:
                                        g.wait_ge(s_pe, pe0 + ct - 1)
                                    elif pd_prev:
                                        g.wait_ge(s_pe, pd_prev)
                                    C.inc(g.dma_start(
                                        ws1[:, (ct % 2) * HD:(ct % 2 + 1) * HD],
                                        wmat[bass.ts(ct, 128), :]),
                                        s_wA[ct % 2], NA[ct % 2], 16)

                            @blk.tensor
                            def _(t):
                                for ct in range(CT):
                                    t.wait_ge(s_wA[ct % 2],
                                              wb[ct % 2] + 16 * (ct // 2 + 1))
                                    if ct == 0:
                                        t.wait_ge(s_dve, ap_d)
                                        if ev_prev:
                                            t.wait_ge(s_dve, ev_prev)
                                    slab = ws1[:, (ct % 2) * HD:(ct % 2 + 1) * HD]
                                    if kind in ("k", "q"):
                                        for dt in range(DT):
                                            last = t.matmul(
                                                ps_p[:, bass.ts(dt, W)],
                                                slab[:, bass.ts(dt, 128)],
                                                xg[:, bass.ts(ct, W)],
                                                start=(ct == 0), stop=(ct == CT - 1))
                                    else:
                                        for ti in range(4):
                                            for h in range(2):
                                                last = t.matmul(
                                                    ps_p[:, ti * HD + h * W:
                                                         ti * HD + (h + 1) * W],
                                                    xg[:, grp * 0 + ct * W + ti * 128:
                                                       ct * W + (ti + 1) * 128],
                                                    slab[:, h * W:(h + 1) * W],
                                                    start=(ct == 0), stop=(ct == CT - 1))
                                    C.inc(last, s_pe, "s_pe")
                            pe_done = C.get("s_pe")

                            if kind == "k":
                                @blk.vector
                                def _(v):
                                    v.wait_ge(s_pe, pe_done)
                                    for dt in range(DT):
                                        C.inc(v.tensor_scalar_add(
                                            k_sb[:, dt * SEQ + grp * W:
                                                 dt * SEQ + (grp + 1) * W],
                                            ps_p[:, bass.ts(dt, W)],
                                            c_bqk[:, 8 + dt:9 + dt]),
                                            s_dve, "s_dve")
                            elif kind == "q":
                                @blk.vector
                                def _(v):
                                    v.wait_ge(s_pe, pe_done)
                                    for dt in range(DT):
                                        C.inc(v.tensor_scalar_add(
                                            q_sb[:, dt * TOK + grp * W:
                                                 dt * TOK + (grp + 1) * W],
                                            ps_p[:, bass.ts(dt, W)],
                                            c_bqk[:, dt:dt + 1]),
                                            s_dve, "s_dve")
                            else:
                                @blk.vector
                                def _(v):
                                    v.wait_ge(s_pe, pe_done)
                                    for ti in range(4):
                                        C.inc(v.tensor_add(
                                            v_sb[:, (grp * 4 + ti) * HD:
                                                 (grp * 4 + ti + 1) * HD],
                                            ps_p[:, bass.ts(ti, HD)], c_bv[:]),
                                            s_dve, "s_dve")
                            pass_done = C.get("s_pe")
                            evac_done = C.get("s_dve")

                    if "k" in S1_PASSES:
                        proj_pass(wk_a, "k")
                    if "v" in S1_PASSES:
                        proj_pass(wv_a, "v")
                    if grp < 2 and "q" in S1_PASSES:
                        proj_pass(wq_a, "q")

                st_xg.close()
                p_sl = st_s1.enter_context(sbuf_t("p_sl", [128, 4 * SEQ], BF16))
                pn_sl = st_s1.enter_context(sbuf_t("pn_sl", [128, 4 * SEQ], BF16))
                pT_sl = st_s1.enter_context(sbuf_t("pT_sl", [128, 16 * W], BF16))
                smx = st_s1.enter_context(sbuf_t("smx", [128, 8], F32))
                if True:
                    sg_gate_act = None
                    for sg in range(S1_SG):
                        with psum_t("ps_s", [128, 2 * SEQ], F32) as ps_s:
                            a_sm0 = C.get("s_act")
                            ev_d = evac_done
                            sg_act = sg_gate_act
                            for tau in range(4):
                                t1 = sg * 4 + tau
                                psv = ps_s[:, (tau % 2) * SEQ:(tau % 2 + 1) * SEQ]

                                @blk.tensor
                                def _(t):
                                    if tau == 0:
                                        t.wait_ge(s_dve, ev_d)
                                        if sg_act:
                                            t.wait_ge(s_act, sg_act)
                                    if tau >= 2:
                                        t.wait_ge(s_act, a_sm0 + 2 * (tau - 2) + 1)
                                    for dt in range(DT):
                                        for b in range(4):
                                            last = t.matmul(
                                                psv[:, bass.ts(b, W)],
                                                q_sb[:, dt * TOK + t1 * 128:
                                                     dt * TOK + (t1 + 1) * 128],
                                                k_sb[:, dt * SEQ + b * W:
                                                     dt * SEQ + (b + 1) * W],
                                                start=(dt == 0), stop=(dt == DT - 1))
                                    C.inc(last, s_pe, "s_pe")
                                pe_sc = C.get("s_pe")

                                @blk.vector
                                def _(v):
                                    v.wait_ge(s_pe, pe_sc)
                                    mc = C.inc(v.reduce_max(smx[:, tau:tau + 1], psv,
                                                            axis=AX.X), s_dve, "s_dve")
                                    v.wait_ge(s_dve, mc)
                                    C.inc(v.tensor_scalar_mul(
                                        smx[:, tau:tau + 1], smx[:, tau:tau + 1],
                                        -SM_SCALE), s_dve, "s_dve")
                                d_mx = C.get("s_dve")

                                @blk.scalar
                                def _(sc):
                                    sc.wait_ge(s_dve, d_mx)
                                    C.inc(sc.activation(
                                        p_sl[:, tau * SEQ:(tau + 1) * SEQ], psv,
                                        AF.Exp, bias=smx[:, tau:tau + 1],
                                        scale=SM_SCALE), s_act, "s_act")
                                a_exp = C.get("s_act")

                                @blk.vector
                                def _(v):
                                    v.wait_ge(s_act, a_exp)
                                    rsc = C.inc(v.reduce_sum(
                                        smx[:, 4 + tau:5 + tau],
                                        p_sl[:, tau * SEQ:(tau + 1) * SEQ],
                                        axis=AX.X), s_dve, "s_dve")
                                    v.wait_ge(s_dve, rsc)
                                    C.inc(v.reciprocal(smx[:, 4 + tau:5 + tau],
                                                       smx[:, 4 + tau:5 + tau]),
                                          s_dve, "s_dve")
                                d_rs = C.get("s_dve")

                                @blk.scalar
                                def _(sc):
                                    sc.wait_ge(s_dve, d_rs)
                                    C.inc(sc.activation(
                                        pn_sl[:, tau * SEQ:(tau + 1) * SEQ],
                                        p_sl[:, tau * SEQ:(tau + 1) * SEQ],
                                        AF.Copy, scale=smx[:, 4 + tau:5 + tau]),
                                        s_act, "s_act")
                        a_pn_done = C.get("s_act")

                        if "t" not in S1_SGP:
                            continue
                        with psum_t("ps_tr", [128, 2 * 1024], BF16) as ps_tr:
                            a_tr0 = C.get("s_act")
                            p_tr0 = C.get("s_pe")

                            @blk.tensor
                            def _(t):
                                for ti in range(16):
                                    if ti == 0:
                                        t.wait_ge(s_act, a_pn_done)
                                    if ti >= 2:
                                        t.wait_ge(s_act, a_tr0 + ti - 1)
                                    for tau in range(4):
                                        last = t.transpose(
                                            ps_tr[:, (ti % 2) * 1024 + tau * 128:
                                                  (ti % 2) * 1024 + (tau + 1) * 128],
                                            pn_sl[:, tau * SEQ + ti * 128:
                                                  tau * SEQ + (ti + 1) * 128],
                                            c_id[:])
                                    C.inc(last, s_pe, "s_pe")

                            @blk.scalar
                            def _(sc):
                                for ti in range(16):
                                    sc.wait_ge(s_pe, p_tr0 + ti + 1)
                                    C.inc(sc.activation(
                                        pT_sl[:, bass.ts(ti, W)],
                                        ps_tr[:, (ti % 2) * 1024:
                                              (ti % 2) * 1024 + W],
                                        AF.Copy), s_act, "s_act")
                        ptr_done = C.get("s_act")

                        if "a" not in S1_SGP:
                            continue
                        with psum_t("ps_av", [128, 2 * W], F32) as ps_av:
                            a_av0 = C.get("s_act")
                            p_av0 = C.get("s_pe")

                            @blk.tensor
                            def _(t):
                                for dt in range(DT):
                                    if dt == 0:
                                        t.wait_ge(s_act, ptr_done)
                                    if dt >= 2:
                                        t.wait_ge(s_act, a_av0 + dt - 1)
                                    for ti in range(16):
                                        last = t.matmul(
                                            ps_av[:, (dt % 2) * W:(dt % 2 + 1) * W],
                                            v_sb[:, ti * HD + dt * 128:
                                                 ti * HD + (dt + 1) * 128],
                                            pT_sl[:, bass.ts(ti, W)],
                                            start=(ti == 0), stop=(ti == 15))
                                    C.inc(last, s_pe, "s_pe")

                            @blk.scalar
                            def _(sc):
                                for dt in range(DT):
                                    sc.wait_ge(s_pe, p_av0 + dt + 1)
                                    C.inc(sc.activation(
                                        ao_sb[:, dt * TOK + sg * W:
                                             dt * TOK + (sg + 1) * W],
                                        ps_av[:, (dt % 2) * W:(dt % 2 + 1) * W],
                                        AF.Copy), s_act, "s_act")
                        sg_gate_act = C.get("s_act")
                        attnv_done = C.get("s_pe")

                st_s1.close()
                if not S1_OUTPROJ:
                    return
                @blk.gpsimd
                def _(g):
                    g.wait_ge(s_w, C.get("s_w"))
                    g.wait_ge(s_pe, attnv_done)
                    g.wait_ge(s_act, sg_gate_act)
                    g.wait_ge(s_dve, C.get("s_dve"))
                    C.inc(g.dma_start(
                        resid()[:].rearrange("p (ct t) -> p ct t", ct=CT),
                        xT[:, 0:TOK].rearrange("(ct p) t -> p ct t", p=128)),
                        s_w, "s_w", 16)
                w_res = C.get("s_w")

                with psum_t("ps_y", [128, 2 * TOK], F32) as ps_y:
                    wb = [C.get(NA[0]), C.get(NA[1])]
                    pe0 = C.get("s_pe")
                    a0 = C.get("s_act")
                    d0 = C.get("s_dve")
                    ao_done = sg_gate_act
                    av_done = attnv_done

                    @blk.gpsimd
                    def _(g):
                        for ct in range(CT):
                            if ct >= 2:
                                g.wait_ge(s_pe, pe0 + ct - 1)
                            elif av_done:
                                g.wait_ge(s_pe, av_done)
                            C.inc(g.dma_start(
                                ws1[:, (ct % 2) * HD:(ct % 2 + 1) * HD],
                                wo_a[bass.ts(ct, 128), :]),
                                s_wA[ct % 2], NA[ct % 2], 16)

                    @blk.tensor
                    def _(t):
                        for ct in range(CT):
                            t.wait_ge(s_wA[ct % 2], wb[ct % 2] + 16 * (ct // 2 + 1))
                            if ct == 0:
                                t.wait_ge(s_act, ao_done)
                            if ct >= 2:
                                t.wait_ge(s_dve, d0 + ct - 1)
                            slab = ws1[:, (ct % 2) * HD:(ct % 2 + 1) * HD]
                            psv = ps_y[:, (ct % 2) * TOK:(ct % 2 + 1) * TOK]
                            for dt in range(DT):
                                for h in range(2):
                                    last = t.matmul(
                                        psv[:, bass.ts(h, W)],
                                        slab[:, bass.ts(dt, 128)],
                                        ao_sb[:, dt * TOK + h * W:
                                             dt * TOK + (h + 1) * W],
                                        start=(dt == 0), stop=(dt == DT - 1))
                            C.inc(last, s_pe, "s_pe")

                    @blk.vector
                    def _(v):
                        v.wait_ge(s_w, w_res)
                        for ct in range(CT):
                            v.wait_ge(s_pe, pe0 + ct + 1)
                            C.inc(v.scalar_tensor_tensor(
                                resid()[:, bass.ts(ct, TOK)],
                                ps_y[:, (ct % 2) * TOK:(ct % 2 + 1) * TOK],
                                c_bo[:, ct:ct + 1],
                                resid()[:, bass.ts(ct, TOK)],
                                ALU.add, ALU.add), s_dve, "s_dve")
                    bar()

        def emit_ffn(is_last):
            with sbuf_t("xn", [128, CT * W], BF16) as xn, \
                 sbuf_t("hbuf", [128, 22 * W], BF16) as hbuf, \
                 sbuf_t("wsA", [128, 2 * D], BF16) as wsA, \
                 sbuf_t("wsB", [128, 2 * D], BF16) as wsB, \
                 sbuf_t("wsD", [128, 2 * 2816], BF16) as wsD:
                gu_done = None
                down_done = None
                h_done_prev = None
                for chunk in range(2):
                    def rsrc(ct, chunk=chunk):
                        return resid()[:, ct * TOK + chunk * W:
                                       ct * TOK + (chunk + 1) * W]
                    a_bc = emit_ln(rsrc, do_bar=False,
                                   pe_wait=C.get("s_pe") if chunk else None,
                                   dve_wait=C.get("s_dve") if chunk else None)
                    apply_done = emit_ln_apply(
                        rsrc, lambda ct: xn[:, bass.ts(ct, W)], do_bar=False,
                        act_wait=a_bc, pe_wait=C.get("s_pe") if chunk else None)

                    with psum_t("ps_g", [128, 2 * W], F32) as ps_g, \
                         psum_t("ps_u", [128, 2 * W], F32) as ps_u, \
                         psum_t("ps_d", [128, 2 * W], F32) as ps_d:
                        for qi, (f0, nft) in enumerate(FQ):
                            wba = [C.get(NA[0]), C.get(NA[1])]
                            wbb = [C.get(NB[0]), C.get(NB[1])]
                            pe0 = C.get("s_pe")
                            a0 = C.get("s_act")
                            d0 = C.get("s_dve")
                            gu_prev = gu_done
                            h_prev = h_done_prev
                            ap_done = apply_done if qi == 0 else None

                            @blk.gpsimd
                            def _(g):
                                for i in range(nft):
                                    ft = f0 + i
                                    if i >= 2:
                                        g.wait_ge(s_pe, pe0 + i - 1)
                                    elif gu_prev:
                                        g.wait_ge(s_pe, gu_prev)
                                    C.inc(g.dma_start(
                                        wsA[:, (i % 2) * D:(i % 2 + 1) * D],
                                        gate_t[bass.ts(ft, 128), :]),
                                        s_wA[i % 2], NA[i % 2], 16)
                                    C.inc(g.dma_start(
                                        wsB[:, (i % 2) * D:(i % 2 + 1) * D],
                                        up_t[bass.ts(ft, 128), :]),
                                        s_wB[i % 2], NB[i % 2], 16)

                            @blk.tensor
                            def _(t):
                                for i in range(nft):
                                    t.wait_ge(s_wA[i % 2],
                                              wba[i % 2] + 16 * (i // 2 + 1))
                                    t.wait_ge(s_wB[i % 2],
                                              wbb[i % 2] + 16 * (i // 2 + 1))
                                    if i == 0 and ap_done:
                                        t.wait_ge(s_dve, ap_done)
                                    if i >= 2:
                                        t.wait_ge(s_dve, d0 + i - 1)
                                    elif h_prev:
                                        t.wait_ge(s_dve, h_prev)
                                    gsl = wsA[:, (i % 2) * D:(i % 2 + 1) * D]
                                    usl = wsB[:, (i % 2) * D:(i % 2 + 1) * D]
                                    pg = ps_g[:, (i % 2) * W:(i % 2 + 1) * W]
                                    pu = ps_u[:, (i % 2) * W:(i % 2 + 1) * W]
                                    for ct in range(CT):
                                        t.matmul(pg, gsl[:, bass.ts(ct, 128)],
                                                 xn[:, bass.ts(ct, W)],
                                                 start=(ct == 0), stop=(ct == CT - 1))
                                        last = t.matmul(pu, usl[:, bass.ts(ct, 128)],
                                                        xn[:, bass.ts(ct, W)],
                                                        start=(ct == 0),
                                                        stop=(ct == CT - 1))
                                    C.inc(last, s_pe, "s_pe")
                            gu_done = C.get("s_pe")

                            @blk.scalar
                            def _(sc):
                                for i in range(nft):
                                    ft = f0 + i
                                    sc.wait_ge(s_pe, pe0 + i + 1)
                                    if i >= 2:
                                        sc.wait_ge(s_dve, d0 + i - 1)
                                    elif h_prev:
                                        sc.wait_ge(s_dve, h_prev)
                                    C.inc(sc.activation(
                                        t_eA[:, (i % 2) * W:(i % 2 + 1) * W],
                                        ps_g[:, (i % 2) * W:(i % 2 + 1) * W],
                                        AF.Silu, scale=c_gs[:, ft:ft + 1]),
                                        s_act, "s_act")

                            @blk.vector
                            def _(v):
                                for i in range(nft):
                                    ft = f0 + i
                                    v.wait_ge(s_act, a0 + i + 1)
                                    C.inc(v.scalar_tensor_tensor(
                                        hbuf[:, bass.ts(i, W)],
                                        ps_u[:, (i % 2) * W:(i % 2 + 1) * W],
                                        c_us[:, ft:ft + 1],
                                        t_eA[:, (i % 2) * W:(i % 2 + 1) * W],
                                        ALU.mult, ALU.mult), s_dve, "s_dve")
                            h_done_prev = C.get("s_dve")
                            h_done = h_done_prev

                            is_final_q = is_last and qi == len(FQ) - 1
                            wbd = [C.get(ND[0]), C.get(ND[1])]
                            pe0d = C.get("s_pe")
                            d0d = C.get("s_dve")
                            ob = [C.get(NO[0]), C.get(NO[1])]
                            dn_prev = down_done

                            @blk.gpsimd
                            def _(g):
                                for ct in range(CT):
                                    if ct >= 2:
                                        g.wait_ge(s_pe, pe0d + ct - 1)
                                    elif dn_prev:
                                        g.wait_ge(s_pe, dn_prev)
                                    C.inc(g.dma_start(
                                        wsD[:, (ct % 2) * 2816:
                                            (ct % 2) * 2816 + nft * 128],
                                        down_t[bass.ts(ct, 128),
                                               f0 * 128:(f0 + nft) * 128]),
                                        s_wD[ct % 2], ND[ct % 2], 16)

                            @blk.tensor
                            def _(t):
                                for ct in range(CT):
                                    t.wait_ge(s_wD[ct % 2],
                                              wbd[ct % 2] + 16 * (ct // 2 + 1))
                                    if ct == 0:
                                        t.wait_ge(s_dve, h_done)
                                    if ct >= 2:
                                        t.wait_ge(s_dve, d0d + ct - 1)
                                    dsl = wsD[:, (ct % 2) * 2816:(ct % 2 + 1) * 2816]
                                    pd = ps_d[:, (ct % 2) * W:(ct % 2 + 1) * W]
                                    for f in range(nft):
                                        last = t.matmul(pd, dsl[:, bass.ts(f, 128)],
                                                        hbuf[:, bass.ts(f, W)],
                                                        start=(f == 0),
                                                        stop=(f == nft - 1))
                                    C.inc(last, s_pe, "s_pe")
                            down_done = C.get("s_pe")

                            @blk.vector
                            def _(v):
                                for ct in range(CT):
                                    v.wait_ge(s_pe, pe0d + ct + 1)
                                    pd = ps_d[:, (ct % 2) * W:(ct % 2 + 1) * W]
                                    if is_final_q:
                                        if ct >= 2:
                                            v.wait_ge(s_o[ct % 2],
                                                      ob[ct % 2]
                                                      + 16 * ((ct - 2) // 2 + 1))
                                        elif ob[ct % 2]:
                                            v.wait_ge(s_o[ct % 2], ob[ct % 2])
                                        C.inc(v.scalar_tensor_tensor(
                                            t_of[:, (ct % 2) * W:(ct % 2 + 1) * W],
                                            pd, c_ds[:, ct:ct + 1], rsrc(ct),
                                            ALU.mult, ALU.add), s_dve, "s_dve")
                                    else:
                                        C.inc(v.scalar_tensor_tensor(
                                            rsrc(ct), pd, c_ds[:, ct:ct + 1],
                                            rsrc(ct), ALU.mult, ALU.add),
                                            s_dve, "s_dve")
                            d_ev0 = C.get("s_dve") - CT

                            if is_final_q:
                                @blk.sync
                                def _(sy):
                                    for ct in range(CT):
                                        sy.wait_ge(s_dve, d_ev0 + ct + 1)
                                        C.inc(sy.dma_start(
                                            outT[ct * 128:(ct + 1) * 128,
                                                 chunk * W:(chunk + 1) * W],
                                            t_of[:, (ct % 2) * W:(ct % 2 + 1) * W]),
                                            s_o[ct % 2], NO[ct % 2], 16)
                bar()

        def emit_attn_b(is_last=False):
            with sbuf_t("ws2", [128, 2 * HD], BF16) as ws2, \
                 sbuf_t("xn", [128, CT * W], BF16) as xn, \
                 sbuf_t("sig", [128, 8 * TOK], BF16) as sig, \
                 sbuf_t("retr", [128, 8 * TOK], BF16) as retr:
                for chunk in range(2):
                    def rsrc(ct, chunk=chunk):
                        return resid()[:, ct * TOK + chunk * W:
                                     ct * TOK + (chunk + 1) * W]
                    emit_ln(rsrc)
                    emit_ln_apply(rsrc, lambda ct: xn[:, bass.ts(ct, W)])

                    with psum_t("ps_q", [128, 8 * W], F32) as ps_q:
                        wb = [C.get(NA[0]), C.get(NA[1])]
                        pe0 = C.get("s_pe")

                        @blk.gpsimd
                        def _(g):
                            for ct in range(CT):
                                if ct >= 2:
                                    g.wait_ge(s_pe, pe0 + ct - 1)
                                C.inc(g.dma_start(
                                    ws2[:, (ct % 2) * HD:(ct % 2 + 1) * HD],
                                    wq_b[bass.ts(ct, 128), :]),
                                    s_wA[ct % 2], NA[ct % 2], 16)

                        @blk.tensor
                        def _(t):
                            for ct in range(CT):
                                t.wait_ge(s_wA[ct % 2], wb[ct % 2] + 16 * (ct // 2 + 1))
                                slab = ws2[:, (ct % 2) * HD:(ct % 2 + 1) * HD]
                                for ot in range(8):
                                    last = t.matmul(ps_q[:, bass.ts(ot, W)],
                                                    slab[:, bass.ts(ot, 128)],
                                                    xn[:, bass.ts(ct, W)],
                                                    start=(ct == 0),
                                                    stop=(ct == CT - 1))
                                C.inc(last, s_pe, "s_pe")
                        pe_q = C.get("s_pe")
                        q_done = pe_q
                        d_elu0 = C.get("s_dve")
                        a_exp0 = C.get("s_act")

                        @blk.vector
                        def _(v):
                            v.wait_ge(s_pe, pe_q)
                            for ot in range(8):
                                if ot >= 2:
                                    v.wait_ge(s_act, a_exp0 + ot - 1)
                                C.inc(v.tensor_scalar(
                                    t_eA[:, (ot % 2) * W:(ot % 2 + 1) * W],
                                    ps_q[:, bass.ts(ot, W)],
                                    c_bqb[:, ot:ot + 1], 0.0, ALU.add, ALU.min),
                                    s_dve, "s_dve")
                                mx = C.inc(v.tensor_scalar(
                                    t_eC[:, (ot % 2) * W:(ot % 2 + 1) * W],
                                    ps_q[:, bass.ts(ot, W)],
                                    c_bqb[:, ot:ot + 1], 0.0, ALU.add, ALU.max),
                                    s_dve, "s_dve")
                                v.wait_ge(s_act, a_exp0 + ot + 1)
                                v.wait_ge(s_dve, mx)
                                C.inc(v.tensor_add(
                                    sig[:, ot * TOK + chunk * W:
                                        ot * TOK + (chunk + 1) * W],
                                    t_eB[:, (ot % 2) * W:(ot % 2 + 1) * W],
                                    t_eC[:, (ot % 2) * W:(ot % 2 + 1) * W]),
                                    s_dve, "s_dve")

                        @blk.scalar
                        def _(sc):
                            for ot in range(8):
                                sc.wait_ge(s_dve, d_elu0 + 3 * ot + 1)
                                if ot >= 2:
                                    sc.wait_ge(s_dve, d_elu0 + 3 * ot - 3)
                                C.inc(sc.activation(
                                    t_eB[:, (ot % 2) * W:(ot % 2 + 1) * W],
                                    t_eA[:, (ot % 2) * W:(ot % 2 + 1) * W],
                                    AF.Exp), s_act, "s_act")
                        elu_done = C.get("s_dve")

                with psum_t("ps_n", [128, 2 * TOK], F32) as ps_n, \
                     psum_t("ps_dn", [1, TOK], F32) as ps_dn, \
                     psum_t("ps_b", [128, TOK], F32) as ps_b, \
                     sbuf_t("den_r", [1, TOK], F32) as den_r, \
                     sbuf_t("den_rb", [1, TOK], BF16) as den_rb, \
                     sbuf_t("rb_s", [128, TOK], BF16) as rb_s:
                    d_retr = C.get("s_dve")
                    sig_done = elu_done
                    for hd in range(8):
                        @blk.tensor
                        def _(t):
                            if hd == 0:
                                t.wait_ge(s_dve, sig_done)
                            if hd >= 1:
                                t.wait_ge(s_dve, d_retr)
                            for h in range(2):
                                t.matmul(ps_n[:, (hd % 2) * TOK + h * W:
                                              (hd % 2) * TOK + (h + 1) * W],
                                         c_m[:, bass.ts(hd, 128)],
                                         sig[:, hd * TOK + h * W:
                                             hd * TOK + (h + 1) * W],
                                         start=True, stop=True)
                            for h in range(2):
                                last = t.matmul(ps_dn[:, bass.ts(h, W)],
                                                c_z[:, hd:hd + 1],
                                                sig[:, hd * TOK + h * W:
                                                    hd * TOK + (h + 1) * W],
                                                start=True, stop=True)
                            C.inc(last, s_pe, "s_pe")
                        pe_nd = C.get("s_pe")

                        @blk.scalar
                        def _(sc):
                            sc.wait_ge(s_pe, pe_nd)
                            C.inc(sc.activation(den_r[:], ps_dn[:], AF.Copy,
                                                bias=EPS_B), s_act, "s_act")
                        a_den = C.get("s_act")

                        @blk.vector
                        def _(v):
                            v.wait_ge(s_act, a_den)
                            rcp = C.inc(v.reciprocal(den_r[:], den_r[:]),
                                        s_dve, "s_dve")
                            v.wait_ge(s_dve, rcp)
                            C.inc(v.tensor_copy(den_rb[:], den_r[:]), s_dve, "s_dve")
                        d_den = C.get("s_dve")

                        @blk.tensor
                        def _(t):
                            t.wait_ge(s_dve, d_den)
                            for h in range(2):
                                last = t.matmul(ps_b[:, bass.ts(h, W)], c_ones_r[:],
                                                den_rb[:, bass.ts(h, W)],
                                                start=True, stop=True)
                            C.inc(last, s_pe, "s_pe")
                        pe_b = C.get("s_pe")

                        @blk.scalar
                        def _(sc):
                            sc.wait_ge(s_pe, pe_b)
                            C.inc(sc.activation(rb_s[:], ps_b[:], AF.Copy),
                                  s_act, "s_act")
                        a_rb = C.get("s_act")

                        @blk.vector
                        def _(v):
                            v.wait_ge(s_act, a_rb)
                            C.inc(v.tensor_mul(retr[:, bass.ts(hd, TOK)],
                                               ps_n[:, bass.ts(hd % 2, TOK)],
                                               rb_s[:]), s_dve, "s_dve")
                        d_retr = C.get("s_dve")

                with sbuf_t("wfo", [128, 2 * D], F32) as wfo, \
                     sbuf_t("t01", [128, HD], F32) as t01, \
                     sbuf_t("eff", [128, 2 * HD], BF16) as eff, \
                     psum_t("ps_y", [128, 2 * TOK], F32) as ps_y:
                    wb = [C.get(NA[0]), C.get(NA[1])]
                    pe0 = C.get("s_pe")
                    d0 = C.get("s_dve")
                    fold_done = [0] * CT
                    add_done = [0] * CT

                    @blk.vector
                    def _(v):
                        def emit_fold(ct):
                            v.wait_ge(s_wA[ct % 2], wb[ct % 2] + 16 * (ct // 2 + 1))
                            if ct >= 1:
                                v.wait_ge(s_dve, fold_done[ct - 1])
                            if ct >= 2:
                                v.wait_ge(s_pe, pe0 + ct - 1)
                            sl = wfo[:, (ct % 2) * D:(ct % 2 + 1) * D]
                            sl4 = sl.rearrange("p (og j e) -> p og j e", og=8, j=4)
                            ta3 = t01[:].rearrange("p (og e) -> p og e", og=8)
                            eb = eff[:, (ct % 2) * HD:(ct % 2 + 1) * HD]
                            eb3 = eb.rearrange("p (og e) -> p og e", og=8)
                            v.tensor_add(ta3, sl4[:, :, 0, :], sl4[:, :, 1, :])
                            p2 = C.inc(v.tensor_add(eb3, sl4[:, :, 2, :],
                                                    sl4[:, :, 3, :]), s_dve, "s_dve")
                            v.wait_ge(s_dve, p2)
                            fold_done[ct] = C.inc(v.tensor_add(eb3, eb3, ta3),
                                                  s_dve, "s_dve")

                        def emit_add(ct):
                            v.wait_ge(s_pe, pe0 + ct + 1)
                            add_done[ct] = C.inc(v.tensor_add(
                                resid()[:, bass.ts(ct, TOK)],
                                ps_y[:, (ct % 2) * TOK:(ct % 2 + 1) * TOK],
                                resid()[:, bass.ts(ct, TOK)]), s_dve, "s_dve")

                        emit_fold(0)
                        emit_fold(1)
                        for ct in range(2, CT):
                            emit_fold(ct)
                            emit_add(ct - 2)
                        emit_add(CT - 2)
                        emit_add(CT - 1)

                    @blk.gpsimd
                    def _(g):
                        for ct in range(CT):
                            if ct >= 2:
                                g.wait_ge(s_dve, fold_done[ct - 2])
                            else:
                                g.wait_ge(s_pe, pe0)
                                g.wait_ge(s_dve, d0)
                            C.inc(g.dma_start(
                                wfo[:, (ct % 2) * D:(ct % 2 + 1) * D],
                                wo_b[bass.ts(ct, 128), :]),
                                s_wA[ct % 2], NA[ct % 2], 16)

                    @blk.tensor
                    def _(t):
                        for ct in range(CT):
                            t.wait_ge(s_dve, max(fold_done[ct],
                                                 add_done[ct - 2] if ct >= 2 else 0))
                            eb = eff[:, (ct % 2) * HD:(ct % 2 + 1) * HD]
                            psv = ps_y[:, (ct % 2) * TOK:(ct % 2 + 1) * TOK]
                            for ot in range(8):
                                for h in range(2):
                                    last = t.matmul(psv[:, bass.ts(h, W)],
                                                    eb[:, bass.ts(ot, 128)],
                                                    retr[:, ot * TOK + h * W:
                                                         ot * TOK + (h + 1) * W],
                                                    start=(ot == 0), stop=(ot == 7))
                            C.inc(last, s_pe, "s_pe")
                    bar()

        def emit_dump_resid():
            d0 = C.get("s_dve")
            ob = [C.get(NO[0]), C.get(NO[1])]

            @blk.vector
            def _(v):
                for ct in range(CT):
                    for h in range(2):
                        i = 2 * ct + h
                        if i >= 2:
                            v.wait_ge(s_o[i % 2],
                                      ob[i % 2] + 16 * ((i - 2) // 2 + 1))
                        elif ob[i % 2]:
                            v.wait_ge(s_o[i % 2], ob[i % 2])
                        C.inc(v.tensor_copy(
                            t_of[:, (i % 2) * W:(i % 2 + 1) * W],
                            resid()[:, ct * TOK + h * W:ct * TOK + (h + 1) * W]),
                            s_dve, "s_dve")

            @blk.sync
            def _(sy):
                for ct in range(CT):
                    for h in range(2):
                        i = 2 * ct + h
                        sy.wait_ge(s_dve, d0 + i + 1)
                        C.inc(sy.dma_start(
                            outT[ct * 128:(ct + 1) * 128, h * W:(h + 1) * W],
                            t_of[:, (i % 2) * W:(i % 2 + 1) * W]),
                            s_o[i % 2], NO[i % 2], 16)
            bar()

        try:
            if upto >= 1:
                emit_s1()
            if upto >= 2:
                emit_ffn(is_last=(upto == 2))
            if upto >= 3:
                emit_attn_b()
            if upto >= 4:
                emit_ffn(is_last=True)
            if upto in (1, 3):
                emit_dump_resid()
        except _Stop:
            pass

        @blk.sync
        def _(sy):
            sy.wait_ge(s_c, C.get("s_c"))
            sy.wait_ge(s_o[0], C.get(NO[0]))
            sy.wait_ge(s_o[1], C.get(NO[1]))

        @blk.gpsimd
        def _(g):
            g.wait_ge(s_w, C.get("s_w"))
            for se, nm in ((s_wA[0], NA[0]), (s_wA[1], NA[1]),
                           (s_wB[0], NB[0]), (s_wB[1], NB[1]),
                           (s_wD[0], ND[0]), (s_wD[1], ND[1])):
                g.wait_ge(se, C.get(nm))

    return nc


def _host_prep(inputs):
    x = np.asarray(inputs["L0_attn_a_x"])
    XT = np.ascontiguousarray(x.reshape(8192, D).T)

    wqkv = np.asarray(inputs["L0_attn_a_Wqkv"])
    wq_a = np.ascontiguousarray(wqkv[:, 0:HD])
    wk_a = np.ascontiguousarray(wqkv[:, HD:2 * HD])
    wv_a = np.ascontiguousarray(wqkv[:, 2 * HD:3 * HD])
    bqkv = np.asarray(inputs["L0_attn_a_bqkv"])
    bqk_t = np.ascontiguousarray(
        np.concatenate([bqkv[0:HD].reshape(8, 128).T,
                        bqkv[HD:2 * HD].reshape(8, 128).T], axis=1))
    bv_row = np.ascontiguousarray(bqkv[2 * HD:3 * HD].reshape(1, HD))
    wo_a = np.ascontiguousarray(
        np.asarray(inputs["L0_attn_a_Wo"])
        .reshape(DT, 128, CT, 128).transpose(2, 1, 0, 3).reshape(CT * 128, HD))
    bo_t = np.ascontiguousarray(np.asarray(inputs["L0_attn_a_bo"]).reshape(CT, 128).T)

    def i8(w):
        return np.asarray(w).view(np.int8)[..., ::4]

    def gup_tile(wm):
        wt = i8(wm).T
        return np.ascontiguousarray(
            wt.reshape(CT, 128, FT, 128).transpose(2, 1, 0, 3).reshape(FT * 128, D))

    gate_t = gup_tile(inputs["L0_ffn_gate_w"])
    up_t = gup_tile(inputs["L0_ffn_up_w"])
    down_t = np.ascontiguousarray(
        i8(inputs["L0_ffn_down_w"]).T
        .reshape(FT, 128, CT, 128).transpose(2, 1, 0, 3).reshape(CT * 128, DFF))
    gs_t = np.ascontiguousarray(np.asarray(inputs["L0_ffn_gate_s"]).reshape(FT, 128).T)
    us_t = np.ascontiguousarray(np.asarray(inputs["L0_ffn_up_s"]).reshape(FT, 128).T)
    ds_t = np.ascontiguousarray(np.asarray(inputs["L0_ffn_down_s"]).reshape(CT, 128).T)

    wq_sel = np.asarray(inputs["L0_attn_b_Wq"]).reshape(32, 128, D)[::4]
    wq_b = np.ascontiguousarray(wq_sel.reshape(8 * 128, D).T)
    bq_b = np.ascontiguousarray(
        np.asarray(inputs["L0_attn_b_bq"]).reshape(32, 128)[::4].T)
    wo_b = np.ascontiguousarray(
        np.asarray(inputs["L0_attn_b_Wo"]).T
        .reshape(32, 128, CT, 128).transpose(2, 1, 0, 3).reshape(CT * 128, D))
    m_t = np.ascontiguousarray(
        np.asarray(inputs["L0_attn_b_M"]).transpose(1, 0, 2).reshape(128, 8 * 128))
    z_t = np.ascontiguousarray(np.asarray(inputs["L0_attn_b_z"]).T)

    shared = dict(
        wq_a=wq_a, wk_a=wk_a, wv_a=wv_a, bqk_t=bqk_t, bv_row=bv_row,
        wo_a=wo_a, bo_t=bo_t, gate_t=gate_t, up_t=up_t, down_t=down_t,
        gs_t=gs_t, us_t=us_t, ds_t=ds_t, wq_b=wq_b, bq_b=bq_b, wo_b=wo_b,
        m_t=m_t, z_t=z_t,
        cvals=np.tile(np.array([[EPS_LN, EPS_B, SM_SCALE, 1.0 / D]],
                      np.float32), (128, 1)),
        ones_col=np.ones((128, 1), dtype=BF),
        ones_row=np.ones((1, 128), dtype=BF),
        ident=np.eye(128, dtype=BF),
    )
    in_maps = []
    for c in range(8):
        b, own = c // 2, c % 2
        bcols = XT[:, b * SEQ:(b + 1) * SEQ]
        xTc = np.ascontiguousarray(
            np.concatenate([bcols[:, own * TOK:(own + 1) * TOK],
                            bcols[:, (1 - own) * TOK:(2 - own) * TOK]], axis=1))
        m = dict(shared)
        m["xT"] = xTc
        in_maps.append(m)
    return in_maps


def kernel(**inputs):
    in_maps = _host_prep(inputs)
    nc = build_program(upto=4)
    res = run_bass_kernel_spmd(nc, in_maps, core_ids=list(range(8)))
    outT = np.concatenate([np.asarray(res.results[c]["outT"]) for c in range(8)],
                          axis=1)
    return np.ascontiguousarray(outT.T).reshape(4, SEQ, D).astype(np.float32)


# version 05
# speedup vs baseline: 1.0346x; 1.0346x over previous
import contextlib

import numpy as np
import ml_dtypes
import concourse.bass as bass
import concourse.mybir as mybir
from concourse.bass_utils import run_bass_kernel_spmd

F32 = mybir.dt.float32
BF16 = mybir.dt.bfloat16
I8 = mybir.dt.int8
AF = mybir.ActivationFunctionType
ALU = mybir.AluOpType
AX = mybir.AxisListType
BF = ml_dtypes.bfloat16

D = 4096
CT = 32
SEQ = 2048
TOK = 1024
DFF = 11008
FT = 86
W = 512
HD = 1024
DT = 8
EPS_LN = 1e-5
EPS_B = 1e-6
SM_SCALE = 1.0 / 32.0
FQ = [(0, 22), (22, 21), (43, 22), (65, 21)]

S1_GROUPS = 4
S1_SG = 2
S1_OUTPROJ = True
S1_PASSES = ("k", "v", "q")
S1_SGP = ("s", "t", "a")


class Counts:
    def __init__(self):
        self.c = {}

    def inc(self, inst, sem, name, n=1):
        inst.then_inc(sem, n)
        self.c[name] = self.c.get(name, 0) + n
        return self.c[name]

    def get(self, name):
        return self.c.get(name, 0)


class _Stop(Exception):
    pass


def build_program(upto=4, trunc=None):
    nc = bass.Bass("TRN2", target_bir_lowering=False)
    global _last_nc
    _last_nc = nc

    def din(name, shape, dtype):
        return nc.dram_tensor(name, shape, dtype, kind="ExternalInput").ap()

    xT = din("xT", [D, SEQ], F32)
    wq_a = din("wq_a", [D, HD], F32)
    wk_a = din("wk_a", [D, HD], F32)
    wv_a = din("wv_a", [D, HD], F32)
    bqk_t = din("bqk_t", [128, 16], F32)
    bv_row = din("bv_row", [1, HD], F32)
    wo_a = din("wo_a", [CT * 128, HD], F32)
    bo_t = din("bo_t", [128, CT], F32)
    gate_t = din("gate_t", [FT * 128, D], I8)
    up_t = din("up_t", [FT * 128, D], I8)
    down_t = din("down_t", [CT * 128, DFF], I8)
    gs_t = din("gs_t", [128, FT], F32)
    us_t = din("us_t", [128, FT], F32)
    ds_t = din("ds_t", [128, CT], F32)
    wq_b = din("wq_b", [D, HD], F32)
    bq_b = din("bq_b", [128, 8], F32)
    wo_b = din("wo_b", [CT * 128, D], F32)
    m_t = din("m_t", [128, 8 * 128], F32)
    z_t = din("z_t", [128, 8], F32)
    cvals = din("cvals", [128, 4], F32)
    ones_col = din("ones_col", [128, 1], BF16)
    ones_row = din("ones_row", [1, 128], BF16)
    ident = din("ident", [128, 128], BF16)
    outT = nc.dram_tensor("outT", [D, TOK], F32, kind="ExternalOutput").ap()

    C = Counts()

    with contextlib.ExitStack() as st:
        _uid = [0]

        def _mang(name):
            _uid[0] += 1
            return f"{name}_{_uid[0]}"

        def sbuf_t(name, shape, dtype):
            return nc.sbuf_tensor(_mang(name), shape, dtype)

        def psum_t(name, shape, dtype):
            return nc.psum_tensor(_mang(name), shape, dtype)

        def sb(name, shape, dtype):
            return st.enter_context(nc.sbuf_tensor(name, shape, dtype))

        c_bqk = sb("c_bqk", [128, 16], F32)
        c_bo = sb("c_bo", [128, CT], F32)
        c_gs = sb("c_gs", [128, FT], F32)
        c_us = sb("c_us", [128, FT], F32)
        c_ds = sb("c_ds", [128, CT], F32)
        c_bqb = sb("c_bqb", [128, 8], F32)
        c_ones_c = sb("c_ones_c", [128, 1], BF16)
        c_ones_r = sb("c_ones_r", [1, 128], BF16)
        c_id = sb("c_id", [128, 128], BF16)
        c_bv = sb("c_bv", [128, HD], BF16)
        c_m = sb("c_m", [128, 8 * 128], BF16)
        c_z = sb("c_z", [128, 8], BF16)
        sq = sb("sq", [128, 2 * W], BF16)
        ln_rows = sb("ln_rows", [1, 4 * W], F32)
        ln_rbf = sb("ln_rbf", [1, 2 * W], BF16)
        m_b = sb("m_b", [128, W], BF16)
        r_b = sb("r_b", [128, W], BF16)
        t_ln = sb("t_ln", [128, 2 * W], BF16)
        t_eA = sb("t_eA", [128, 2 * W], BF16)
        t_eB = sb("t_eB", [128, 2 * W], BF16)
        t_eC = sb("t_eC", [128, 2 * W], BF16)
        t_y = sb("t_y", [128, 2 * TOK], BF16)
        t_of = sb("t_of", [128, 2 * W], F32)
        c_cv = sb("c_cv", [128, 4], F32)
        for _i, _v in enumerate((EPS_LN, EPS_B, SM_SCALE, 1.0 / D)):
            nc.const_aps.register(F32, _v, c_cv[:, _i:_i + 1])
        s_w = st.enter_context(nc.semaphore("s_w"))
        s_wA = [st.enter_context(nc.semaphore("s_wA0")),
                st.enter_context(nc.semaphore("s_wA1"))]
        s_wB = [st.enter_context(nc.semaphore("s_wB0")),
                st.enter_context(nc.semaphore("s_wB1"))]
        s_wD = [st.enter_context(nc.semaphore("s_wD0")),
                st.enter_context(nc.semaphore("s_wD1"))]
        s_o = [st.enter_context(nc.semaphore("s_o0")),
               st.enter_context(nc.semaphore("s_o1"))]
        NA = ("s_wA0", "s_wA1")
        NB = ("s_wB0", "s_wB1")
        ND = ("s_wD0", "s_wD1")
        NO = ("s_o0", "s_o1")
        s_c = st.enter_context(nc.semaphore("s_c"))
        s_pe = st.enter_context(nc.semaphore("s_pe"))
        s_act = st.enter_context(nc.semaphore("s_act"))
        s_dve = st.enter_context(nc.semaphore("s_dve"))
        _blk_holder = [nc.Block().__enter__()]
        st.callback(lambda: _blk_holder[0].__exit__(None, None, None))

        class _BlkProxy:
            def tensor(self, f):
                return _blk_holder[0].tensor(f)

            def scalar(self, f):
                return _blk_holder[0].scalar(f)

            def vector(self, f):
                return _blk_holder[0].vector(f)

            def gpsimd(self, f):
                return _blk_holder[0].gpsimd(f)

            def sync(self, f):
                return _blk_holder[0].sync(f)

        blk = _BlkProxy()
        buf = {"resid": None}

        def resid():
            if buf["resid"] is None:
                buf["resid"] = st.enter_context(
                    nc.sbuf_tensor("resid", [128, CT * TOK], BF16, side="right"))
            return buf["resid"]

        _barn = [0]

        def bar():
            _blk_holder[0].__exit__(None, None, None)
            _blk_holder[0] = nc.Block().__enter__()
            _barn[0] += 1
            if trunc is not None and _barn[0] >= trunc:
                raise _Stop()
            tgt = [(s_pe, C.get("s_pe")), (s_act, C.get("s_act")),
                   (s_dve, C.get("s_dve"))]
            for deco in (blk.tensor, blk.scalar, blk.vector, blk.gpsimd,
                         blk.sync):
                @deco
                def _(e):
                    for se, val in tgt:
                        if val:
                            e.wait_ge(se, val)

        @blk.sync
        def _(sy):
            for dst, src in (
                (c_bqk, bqk_t), (c_bo, bo_t), (c_gs, gs_t), (c_us, us_t),
                (c_ds, ds_t), (c_bqb, bq_b), (c_cv, cvals), (c_ones_c, ones_col),
                (c_ones_r, ones_row), (c_id, ident),
            ):
                C.inc(sy.dma_start(dst[:], src[:]), s_c, "s_c", 16)

        @blk.gpsimd
        def _(g):
            C.inc(g.dma_start(c_m[:], m_t[:]), s_w, "s_w", 16)
            C.inc(g.dma_start(c_z[:], z_t[:]), s_w, "s_w", 16)

        with sbuf_t("bv_r", [1, HD], F32) as bv_r, \
             sbuf_t("bv_rb", [1, HD], BF16) as bv_rb, \
             psum_t("pb0", [128, HD], F32) as pb0:
            @blk.sync
            def _(sy):
                C.inc(sy.dma_start(bv_r[:], bv_row[:]), s_c, "s_c", 16)

            @blk.vector
            def _(v):
                v.wait_ge(s_c, C.get("s_c"))
                C.inc(v.tensor_copy(bv_rb[:], bv_r[:]), s_dve, "s_dve")

            @blk.tensor
            def _(t):
                t.wait_ge(s_dve, C.get("s_dve"))
                t.wait_ge(s_w, C.get("s_w"))
                for h in range(2):
                    last = t.matmul(pb0[:, h * W:(h + 1) * W], c_ones_r[:],
                                    bv_rb[:, h * W:(h + 1) * W], start=True, stop=True)
                C.inc(last, s_pe, "s_pe")

            @blk.scalar
            def _(sc):
                sc.wait_ge(s_pe, C.get("s_pe"))
                C.inc(sc.copy(c_bv[:], pb0[:]), s_act, "s_act")
            for deco in (blk.tensor, blk.scalar, blk.vector, blk.gpsimd,
                         blk.sync):
                @deco
                def _(e):
                    e.wait_ge(s_c, C.get("s_c"))
                    e.wait_ge(s_w, C.get("s_w"))
            bar()

        def emit_ln(src_of_ct, do_bar=True, pe_wait=None, dve_wait=None):
            with psum_t("ps_sum", [1, W], F32) as ps_sum, \
                 psum_t("ps_sq", [1, W], F32) as ps_sq, \
                 psum_t("ps_bc", [128, 2 * W], F32) as ps_bc:
                a0 = C.get("s_act")
                p0 = C.get("s_pe")

                @blk.scalar
                def _(sc):
                    if pe_wait:
                        sc.wait_ge(s_pe, pe_wait)
                    if dve_wait:
                        sc.wait_ge(s_dve, dve_wait)
                    for ct in range(CT):
                        if ct >= 2:
                            sc.wait_ge(s_pe, p0 + ct - 1)
                        C.inc(sc.activation(sq[:, (ct % 2) * W:(ct % 2 + 1) * W],
                                            src_of_ct(ct), AF.Square), s_act, "s_act")

                @blk.tensor
                def _(t):
                    if pe_wait is None and dve_wait:
                        t.wait_ge(s_dve, dve_wait)
                    for ct in range(CT):
                        t.wait_ge(s_act, a0 + ct + 1)
                        t.matmul(ps_sum[:], c_ones_c[:], src_of_ct(ct),
                                 start=(ct == 0), stop=(ct == CT - 1))
                        C.inc(t.matmul(ps_sq[:], c_ones_c[:],
                                       sq[:, (ct % 2) * W:(ct % 2 + 1) * W],
                                       start=(ct == 0), stop=(ct == CT - 1)),
                              s_pe, "s_pe")

                m_row = ln_rows[:, 0:W]
                msq_row = ln_rows[:, W:2 * W]
                var_row = ln_rows[:, 2 * W:3 * W]
                sd_row = ln_rows[:, 3 * W:4 * W]
                m_rb = ln_rbf[:, 0:W]
                rs_rb = ln_rbf[:, W:2 * W]
                pe_stats = C.get("s_pe")

                @blk.vector
                def _(v):
                    v.wait_ge(s_pe, pe_stats)
                    C.inc(v.tensor_scalar_mul(m_row, ps_sum[:], 1.0 / D),
                          s_dve, "s_dve")

                @blk.scalar
                def _(sc):
                    sc.wait_ge(s_pe, pe_stats)
                    C.inc(sc.activation(msq_row, ps_sum[:], AF.Square, scale=1.0 / D),
                          s_act, "s_act")
                a_msq = C.get("s_act")

                @blk.vector
                def _(v):
                    v.wait_ge(s_act, a_msq)
                    C.inc(v.scalar_tensor_tensor(var_row, ps_sq[:], 1.0 / D, msq_row,
                                                 ALU.mult, ALU.subtract),
                          s_dve, "s_dve")
                d_var = C.get("s_dve")

                @blk.scalar
                def _(sc):
                    sc.wait_ge(s_dve, d_var)
                    C.inc(sc.activation(sd_row, var_row, AF.Sqrt, bias=EPS_LN),
                          s_act, "s_act")
                a_sd = C.get("s_act")

                @blk.vector
                def _(v):
                    v.wait_ge(s_act, a_sd)
                    rc = C.inc(v.reciprocal(sd_row, sd_row), s_dve, "s_dve")
                    v.tensor_copy(m_rb, m_row)
                    v.wait_ge(s_dve, rc)
                    C.inc(v.tensor_copy(rs_rb, sd_row), s_dve, "s_dve")
                d_rows = C.get("s_dve")

                @blk.tensor
                def _(t):
                    t.wait_ge(s_dve, d_rows)
                    t.matmul(ps_bc[:, 0:W], c_ones_r[:], m_rb, start=True, stop=True)
                    C.inc(t.matmul(ps_bc[:, W:2 * W], c_ones_r[:], rs_rb,
                                   start=True, stop=True), s_pe, "s_pe")
                pe_bc = C.get("s_pe")

                @blk.scalar
                def _(sc):
                    sc.wait_ge(s_pe, pe_bc)
                    sc.copy(m_b[:], ps_bc[:, 0:W])
                    C.inc(sc.copy(r_b[:], ps_bc[:, W:2 * W]), s_act, "s_act")
                if do_bar:
                    bar()
                return C.get("s_act")

        def emit_ln_apply(src_of_ct, dst_of_ct, do_bar=True, act_wait=None,
                          pe_wait=None):
            done = [0] * CT

            @blk.vector
            def _(v):
                if act_wait:
                    v.wait_ge(s_act, act_wait)
                if pe_wait:
                    v.wait_ge(s_pe, pe_wait)
                for ct in range(CT):
                    tv = t_ln[:, (ct % 2) * W:(ct % 2 + 1) * W]
                    sc_ = C.inc(v.tensor_sub(tv, src_of_ct(ct), m_b[:]),
                                s_dve, "s_dve")
                    v.wait_ge(s_dve, sc_)
                    done[ct] = C.inc(v.tensor_mul(dst_of_ct(ct), tv, r_b[:]),
                                     s_dve, "s_dve")
            if do_bar:
                bar()
            return done

        def emit_s1():
            with sbuf_t("ws1", [128, 2 * HD], BF16) as ws1, \
                 sbuf_t("ao_sb", [128, DT * TOK], BF16) as ao_sb, \
                 contextlib.ExitStack() as st_s1:
                k_sb = st_s1.enter_context(sbuf_t("k_sb", [128, DT * SEQ], BF16))
                v_sb = st_s1.enter_context(sbuf_t("v_sb", [128, 16 * HD], BF16))
                q_sb = st_s1.enter_context(sbuf_t("q_sb", [128, DT * TOK], BF16))
                st_xg = contextlib.ExitStack()
                xg = st_xg.enter_context(sbuf_t("xg", [128, CT * W], BF16))
                pass_done = None
                evac_done = None
                for grp in range(S1_GROUPS):
                    @blk.gpsimd
                    def _(g, pd=pass_done):
                        g.wait_ge(s_w, C.get("s_w"))
                        if pd:
                            g.wait_ge(s_pe, pd)
                        C.inc(g.dma_start(
                            xg[:].rearrange("p (ct t) -> p ct t", ct=CT),
                            xT[:, grp * W:(grp + 1) * W]
                            .rearrange("(ct p) t -> p ct t", p=128)),
                            s_w, "s_w", 16)
                    wv_xg = C.get("s_w")
                    for deco in (blk.scalar, blk.tensor, blk.vector):
                        @deco
                        def _(e):
                            e.wait_ge(s_w, wv_xg)
                    a_bc = emit_ln(lambda ct: xg[:, ct * W:(ct + 1) * W],
                                   do_bar=False, pe_wait=pass_done,
                                   dve_wait=evac_done)
                    apply_ct = emit_ln_apply(
                        lambda ct: xg[:, ct * W:(ct + 1) * W],
                        lambda ct: xg[:, ct * W:(ct + 1) * W],
                        do_bar=False, act_wait=a_bc, pe_wait=pass_done)
                    apply_done = apply_ct[-1]

                    def proj_pass(wmat, kind, grp=grp):
                        nonlocal pass_done, evac_done
                        with psum_t("ps_p", [128, 8 * W], F32) as ps_p:
                            wb = [C.get(NA[0]), C.get(NA[1])]
                            pe0 = C.get("s_pe")
                            pd_prev = pass_done
                            ev_prev = evac_done
                            ap_d = apply_done
                            ap_ct = apply_ct
                            first = (wmat is wk_a)

                            @blk.gpsimd
                            def _(g):
                                for ct in range(CT):
                                    if ct >= 2:
                                        g.wait_ge(s_pe, pe0 + ct - 1)
                                    elif pd_prev:
                                        g.wait_ge(s_pe, pd_prev)
                                    C.inc(g.dma_start(
                                        ws1[:, (ct % 2) * HD:(ct % 2 + 1) * HD],
                                        wmat[bass.ts(ct, 128), :]),
                                        s_wA[ct % 2], NA[ct % 2], 16)

                            @blk.tensor
                            def _(t):
                                for ct in range(CT):
                                    t.wait_ge(s_wA[ct % 2],
                                              wb[ct % 2] + 16 * (ct // 2 + 1))
                                    if first:
                                        t.wait_ge(s_dve, ap_ct[ct])
                                    elif ct == 0:
                                        t.wait_ge(s_dve, ap_d)
                                    if ct == 0 and ev_prev:
                                        t.wait_ge(s_dve, ev_prev)
                                    slab = ws1[:, (ct % 2) * HD:(ct % 2 + 1) * HD]
                                    if kind in ("k", "q"):
                                        for dt in range(DT):
                                            last = t.matmul(
                                                ps_p[:, bass.ts(dt, W)],
                                                slab[:, bass.ts(dt, 128)],
                                                xg[:, bass.ts(ct, W)],
                                                start=(ct == 0), stop=(ct == CT - 1))
                                    else:
                                        for ti in range(4):
                                            for h in range(2):
                                                last = t.matmul(
                                                    ps_p[:, ti * HD + h * W:
                                                         ti * HD + (h + 1) * W],
                                                    xg[:, grp * 0 + ct * W + ti * 128:
                                                       ct * W + (ti + 1) * 128],
                                                    slab[:, h * W:(h + 1) * W],
                                                    start=(ct == 0), stop=(ct == CT - 1))
                                    C.inc(last, s_pe, "s_pe")
                            pe_done = C.get("s_pe")

                            if kind == "k":
                                @blk.vector
                                def _(v):
                                    v.wait_ge(s_pe, pe_done)
                                    for dt in range(DT):
                                        C.inc(v.tensor_scalar_add(
                                            k_sb[:, dt * SEQ + grp * W:
                                                 dt * SEQ + (grp + 1) * W],
                                            ps_p[:, bass.ts(dt, W)],
                                            c_bqk[:, 8 + dt:9 + dt]),
                                            s_dve, "s_dve")
                            elif kind == "q":
                                @blk.vector
                                def _(v):
                                    v.wait_ge(s_pe, pe_done)
                                    for dt in range(DT):
                                        C.inc(v.tensor_scalar_add(
                                            q_sb[:, dt * TOK + grp * W:
                                                 dt * TOK + (grp + 1) * W],
                                            ps_p[:, bass.ts(dt, W)],
                                            c_bqk[:, dt:dt + 1]),
                                            s_dve, "s_dve")
                            else:
                                @blk.vector
                                def _(v):
                                    v.wait_ge(s_pe, pe_done)
                                    for ti in range(4):
                                        C.inc(v.tensor_add(
                                            v_sb[:, (grp * 4 + ti) * HD:
                                                 (grp * 4 + ti + 1) * HD],
                                            ps_p[:, bass.ts(ti, HD)], c_bv[:]),
                                            s_dve, "s_dve")
                            pass_done = C.get("s_pe")
                            evac_done = C.get("s_dve")

                    if "k" in S1_PASSES:
                        proj_pass(wk_a, "k")
                    if "v" in S1_PASSES:
                        proj_pass(wv_a, "v")
                    if grp < 2 and "q" in S1_PASSES:
                        proj_pass(wq_a, "q")

                st_xg.close()
                p_sl = st_s1.enter_context(sbuf_t("p_sl", [128, 4 * SEQ], BF16))
                pn_sl = st_s1.enter_context(sbuf_t("pn_sl", [128, 4 * SEQ], BF16))
                pT_sl = st_s1.enter_context(sbuf_t("pT_sl", [128, 16 * W], BF16))
                smx = st_s1.enter_context(sbuf_t("smx", [128, 8], F32))
                if True:
                    sg_gate_act = None
                    for sg in range(S1_SG):
                        with psum_t("ps_s", [128, 2 * SEQ], F32) as ps_s:
                            a_sm0 = C.get("s_act")
                            ev_d = evac_done
                            sg_act = sg_gate_act
                            for tau in range(4):
                                t1 = sg * 4 + tau
                                psv = ps_s[:, (tau % 2) * SEQ:(tau % 2 + 1) * SEQ]

                                @blk.tensor
                                def _(t):
                                    if tau == 0:
                                        t.wait_ge(s_dve, ev_d)
                                        if sg_act:
                                            t.wait_ge(s_act, sg_act)
                                    if tau >= 2:
                                        t.wait_ge(s_act, a_sm0 + 2 * (tau - 2) + 1)
                                    for dt in range(DT):
                                        for b in range(4):
                                            last = t.matmul(
                                                psv[:, bass.ts(b, W)],
                                                q_sb[:, dt * TOK + t1 * 128:
                                                     dt * TOK + (t1 + 1) * 128],
                                                k_sb[:, dt * SEQ + b * W:
                                                     dt * SEQ + (b + 1) * W],
                                                start=(dt == 0), stop=(dt == DT - 1))
                                    C.inc(last, s_pe, "s_pe")
                                pe_sc = C.get("s_pe")

                                @blk.vector
                                def _(v):
                                    v.wait_ge(s_pe, pe_sc)
                                    mc = C.inc(v.reduce_max(smx[:, tau:tau + 1], psv,
                                                            axis=AX.X), s_dve, "s_dve")
                                    v.wait_ge(s_dve, mc)
                                    C.inc(v.tensor_scalar_mul(
                                        smx[:, tau:tau + 1], smx[:, tau:tau + 1],
                                        -SM_SCALE), s_dve, "s_dve")
                                d_mx = C.get("s_dve")

                                @blk.scalar
                                def _(sc):
                                    sc.wait_ge(s_dve, d_mx)
                                    C.inc(sc.activation(
                                        p_sl[:, tau * SEQ:(tau + 1) * SEQ], psv,
                                        AF.Exp, bias=smx[:, tau:tau + 1],
                                        scale=SM_SCALE), s_act, "s_act")
                                a_exp = C.get("s_act")

                                @blk.vector
                                def _(v):
                                    v.wait_ge(s_act, a_exp)
                                    rsc = C.inc(v.reduce_sum(
                                        smx[:, 4 + tau:5 + tau],
                                        p_sl[:, tau * SEQ:(tau + 1) * SEQ],
                                        axis=AX.X), s_dve, "s_dve")
                                    v.wait_ge(s_dve, rsc)
                                    C.inc(v.reciprocal(smx[:, 4 + tau:5 + tau],
                                                       smx[:, 4 + tau:5 + tau]),
                                          s_dve, "s_dve")
                                d_rs = C.get("s_dve")

                                @blk.scalar
                                def _(sc):
                                    sc.wait_ge(s_dve, d_rs)
                                    C.inc(sc.activation(
                                        pn_sl[:, tau * SEQ:(tau + 1) * SEQ],
                                        p_sl[:, tau * SEQ:(tau + 1) * SEQ],
                                        AF.Copy, scale=smx[:, 4 + tau:5 + tau]),
                                        s_act, "s_act")
                        a_pn_done = C.get("s_act")

                        if "t" not in S1_SGP:
                            continue
                        with psum_t("ps_tr", [128, 2 * 1024], BF16) as ps_tr:
                            a_tr0 = C.get("s_act")
                            p_tr0 = C.get("s_pe")

                            @blk.tensor
                            def _(t):
                                for ti in range(16):
                                    if ti == 0:
                                        t.wait_ge(s_act, a_pn_done)
                                    if ti >= 2:
                                        t.wait_ge(s_act, a_tr0 + ti - 1)
                                    for tau in range(4):
                                        last = t.transpose(
                                            ps_tr[:, (ti % 2) * 1024 + tau * 128:
                                                  (ti % 2) * 1024 + (tau + 1) * 128],
                                            pn_sl[:, tau * SEQ + ti * 128:
                                                  tau * SEQ + (ti + 1) * 128],
                                            c_id[:])
                                    C.inc(last, s_pe, "s_pe")

                            @blk.scalar
                            def _(sc):
                                for ti in range(16):
                                    sc.wait_ge(s_pe, p_tr0 + ti + 1)
                                    C.inc(sc.activation(
                                        pT_sl[:, bass.ts(ti, W)],
                                        ps_tr[:, (ti % 2) * 1024:
                                              (ti % 2) * 1024 + W],
                                        AF.Copy), s_act, "s_act")
                        ptr_done = C.get("s_act")

                        if "a" not in S1_SGP:
                            continue
                        with psum_t("ps_av", [128, 2 * W], F32) as ps_av:
                            a_av0 = C.get("s_act")
                            p_av0 = C.get("s_pe")

                            @blk.tensor
                            def _(t):
                                for dt in range(DT):
                                    if dt == 0:
                                        t.wait_ge(s_act, ptr_done)
                                    if dt >= 2:
                                        t.wait_ge(s_act, a_av0 + dt - 1)
                                    for ti in range(16):
                                        last = t.matmul(
                                            ps_av[:, (dt % 2) * W:(dt % 2 + 1) * W],
                                            v_sb[:, ti * HD + dt * 128:
                                                 ti * HD + (dt + 1) * 128],
                                            pT_sl[:, bass.ts(ti, W)],
                                            start=(ti == 0), stop=(ti == 15))
                                    C.inc(last, s_pe, "s_pe")

                            @blk.scalar
                            def _(sc):
                                for dt in range(DT):
                                    sc.wait_ge(s_pe, p_av0 + dt + 1)
                                    C.inc(sc.activation(
                                        ao_sb[:, dt * TOK + sg * W:
                                             dt * TOK + (sg + 1) * W],
                                        ps_av[:, (dt % 2) * W:(dt % 2 + 1) * W],
                                        AF.Copy), s_act, "s_act")
                        sg_gate_act = C.get("s_act")
                        attnv_done = C.get("s_pe")

                st_s1.close()
                if not S1_OUTPROJ:
                    return
                @blk.gpsimd
                def _(g):
                    g.wait_ge(s_w, C.get("s_w"))
                    g.wait_ge(s_pe, attnv_done)
                    g.wait_ge(s_act, sg_gate_act)
                    g.wait_ge(s_dve, C.get("s_dve"))
                    C.inc(g.dma_start(
                        resid()[:].rearrange("p (ct t) -> p ct t", ct=CT),
                        xT[:, 0:TOK].rearrange("(ct p) t -> p ct t", p=128)),
                        s_w, "s_w", 16)
                w_res = C.get("s_w")

                with psum_t("ps_y", [128, 2 * TOK], F32) as ps_y:
                    wb = [C.get(NA[0]), C.get(NA[1])]
                    pe0 = C.get("s_pe")
                    a0 = C.get("s_act")
                    d0 = C.get("s_dve")
                    ao_done = sg_gate_act
                    av_done = attnv_done

                    @blk.gpsimd
                    def _(g):
                        for ct in range(CT):
                            if ct >= 2:
                                g.wait_ge(s_pe, pe0 + ct - 1)
                            elif av_done:
                                g.wait_ge(s_pe, av_done)
                            C.inc(g.dma_start(
                                ws1[:, (ct % 2) * HD:(ct % 2 + 1) * HD],
                                wo_a[bass.ts(ct, 128), :]),
                                s_wA[ct % 2], NA[ct % 2], 16)

                    @blk.tensor
                    def _(t):
                        for ct in range(CT):
                            t.wait_ge(s_wA[ct % 2], wb[ct % 2] + 16 * (ct // 2 + 1))
                            if ct == 0:
                                t.wait_ge(s_act, ao_done)
                            if ct >= 2:
                                t.wait_ge(s_dve, d0 + ct - 1)
                            slab = ws1[:, (ct % 2) * HD:(ct % 2 + 1) * HD]
                            psv = ps_y[:, (ct % 2) * TOK:(ct % 2 + 1) * TOK]
                            for dt in range(DT):
                                for h in range(2):
                                    last = t.matmul(
                                        psv[:, bass.ts(h, W)],
                                        slab[:, bass.ts(dt, 128)],
                                        ao_sb[:, dt * TOK + h * W:
                                             dt * TOK + (h + 1) * W],
                                        start=(dt == 0), stop=(dt == DT - 1))
                            C.inc(last, s_pe, "s_pe")

                    @blk.vector
                    def _(v):
                        v.wait_ge(s_w, w_res)
                        for ct in range(CT):
                            v.wait_ge(s_pe, pe0 + ct + 1)
                            C.inc(v.scalar_tensor_tensor(
                                resid()[:, bass.ts(ct, TOK)],
                                ps_y[:, (ct % 2) * TOK:(ct % 2 + 1) * TOK],
                                c_bo[:, ct:ct + 1],
                                resid()[:, bass.ts(ct, TOK)],
                                ALU.add, ALU.add), s_dve, "s_dve")
                    bar()

        def emit_ffn(is_last):
            with sbuf_t("xn", [128, CT * W], BF16) as xn, \
                 sbuf_t("hbuf", [128, 22 * W], BF16) as hbuf, \
                 sbuf_t("wsA", [128, 2 * D], BF16) as wsA, \
                 sbuf_t("wsB", [128, 2 * D], BF16) as wsB, \
                 sbuf_t("wsD", [128, 2 * 2816], BF16) as wsD:
                gu_done = None
                down_done = None
                h_done_prev = None
                for chunk in range(2):
                    def rsrc(ct, chunk=chunk):
                        return resid()[:, ct * TOK + chunk * W:
                                       ct * TOK + (chunk + 1) * W]
                    a_bc = emit_ln(rsrc, do_bar=False,
                                   pe_wait=C.get("s_pe") if chunk else None,
                                   dve_wait=C.get("s_dve") if chunk else None)
                    apply_ct = emit_ln_apply(
                        rsrc, lambda ct: xn[:, bass.ts(ct, W)], do_bar=False,
                        act_wait=a_bc, pe_wait=C.get("s_pe") if chunk else None)
                    apply_done = apply_ct[-1]

                    with psum_t("ps_g", [128, 2 * W], F32) as ps_g, \
                         psum_t("ps_u", [128, 2 * W], F32) as ps_u, \
                         psum_t("ps_d", [128, 2 * W], F32) as ps_d:
                        for qi, (f0, nft) in enumerate(FQ):
                            wba = [C.get(NA[0]), C.get(NA[1])]
                            wbb = [C.get(NB[0]), C.get(NB[1])]
                            pe0 = C.get("s_pe")
                            a0 = C.get("s_act")
                            d0 = C.get("s_dve")
                            gu_prev = gu_done
                            h_prev = h_done_prev
                            ap_ct = apply_ct if qi == 0 else None

                            @blk.gpsimd
                            def _(g):
                                for i in range(nft):
                                    ft = f0 + i
                                    if i >= 2:
                                        g.wait_ge(s_pe, pe0 + i - 1)
                                    elif gu_prev:
                                        g.wait_ge(s_pe, gu_prev)
                                    C.inc(g.dma_start(
                                        wsA[:, (i % 2) * D:(i % 2 + 1) * D],
                                        gate_t[bass.ts(ft, 128), :]),
                                        s_wA[i % 2], NA[i % 2], 16)
                                    C.inc(g.dma_start(
                                        wsB[:, (i % 2) * D:(i % 2 + 1) * D],
                                        up_t[bass.ts(ft, 128), :]),
                                        s_wB[i % 2], NB[i % 2], 16)

                            @blk.tensor
                            def _(t):
                                for i in range(nft):
                                    t.wait_ge(s_wA[i % 2],
                                              wba[i % 2] + 16 * (i // 2 + 1))
                                    t.wait_ge(s_wB[i % 2],
                                              wbb[i % 2] + 16 * (i // 2 + 1))
                                    if i >= 2:
                                        t.wait_ge(s_dve, d0 + i - 1)
                                    elif h_prev:
                                        t.wait_ge(s_dve, h_prev)
                                    gsl = wsA[:, (i % 2) * D:(i % 2 + 1) * D]
                                    usl = wsB[:, (i % 2) * D:(i % 2 + 1) * D]
                                    pg = ps_g[:, (i % 2) * W:(i % 2 + 1) * W]
                                    pu = ps_u[:, (i % 2) * W:(i % 2 + 1) * W]
                                    for ct in range(CT):
                                        if i == 0 and ap_ct:
                                            t.wait_ge(s_dve, ap_ct[ct])
                                        t.matmul(pg, gsl[:, bass.ts(ct, 128)],
                                                 xn[:, bass.ts(ct, W)],
                                                 start=(ct == 0), stop=(ct == CT - 1))
                                        last = t.matmul(pu, usl[:, bass.ts(ct, 128)],
                                                        xn[:, bass.ts(ct, W)],
                                                        start=(ct == 0),
                                                        stop=(ct == CT - 1))
                                    C.inc(last, s_pe, "s_pe")
                            gu_done = C.get("s_pe")

                            @blk.scalar
                            def _(sc):
                                for i in range(nft):
                                    ft = f0 + i
                                    sc.wait_ge(s_pe, pe0 + i + 1)
                                    if i >= 2:
                                        sc.wait_ge(s_dve, d0 + i - 1)
                                    elif h_prev:
                                        sc.wait_ge(s_dve, h_prev)
                                    C.inc(sc.activation(
                                        t_eA[:, (i % 2) * W:(i % 2 + 1) * W],
                                        ps_g[:, (i % 2) * W:(i % 2 + 1) * W],
                                        AF.Silu, scale=c_gs[:, ft:ft + 1]),
                                        s_act, "s_act")

                            @blk.vector
                            def _(v):
                                for i in range(nft):
                                    ft = f0 + i
                                    v.wait_ge(s_act, a0 + i + 1)
                                    C.inc(v.scalar_tensor_tensor(
                                        hbuf[:, bass.ts(i, W)],
                                        ps_u[:, (i % 2) * W:(i % 2 + 1) * W],
                                        c_us[:, ft:ft + 1],
                                        t_eA[:, (i % 2) * W:(i % 2 + 1) * W],
                                        ALU.mult, ALU.mult), s_dve, "s_dve")
                            h_done_prev = C.get("s_dve")
                            h_done = h_done_prev

                            is_final_q = is_last and qi == len(FQ) - 1
                            wbd = [C.get(ND[0]), C.get(ND[1])]
                            pe0d = C.get("s_pe")
                            d0d = C.get("s_dve")
                            ob = [C.get(NO[0]), C.get(NO[1])]
                            dn_prev = down_done

                            @blk.gpsimd
                            def _(g):
                                for ct in range(CT):
                                    if ct >= 2:
                                        g.wait_ge(s_pe, pe0d + ct - 1)
                                    elif dn_prev:
                                        g.wait_ge(s_pe, dn_prev)
                                    C.inc(g.dma_start(
                                        wsD[:, (ct % 2) * 2816:
                                            (ct % 2) * 2816 + nft * 128],
                                        down_t[bass.ts(ct, 128),
                                               f0 * 128:(f0 + nft) * 128]),
                                        s_wD[ct % 2], ND[ct % 2], 16)

                            @blk.tensor
                            def _(t):
                                for ct in range(CT):
                                    t.wait_ge(s_wD[ct % 2],
                                              wbd[ct % 2] + 16 * (ct // 2 + 1))
                                    if ct == 0:
                                        t.wait_ge(s_dve, h_done)
                                    if ct >= 2:
                                        t.wait_ge(s_dve, d0d + ct - 1)
                                    dsl = wsD[:, (ct % 2) * 2816:(ct % 2 + 1) * 2816]
                                    pd = ps_d[:, (ct % 2) * W:(ct % 2 + 1) * W]
                                    for f in range(nft):
                                        last = t.matmul(pd, dsl[:, bass.ts(f, 128)],
                                                        hbuf[:, bass.ts(f, W)],
                                                        start=(f == 0),
                                                        stop=(f == nft - 1))
                                    C.inc(last, s_pe, "s_pe")
                            down_done = C.get("s_pe")

                            @blk.vector
                            def _(v):
                                for ct in range(CT):
                                    v.wait_ge(s_pe, pe0d + ct + 1)
                                    pd = ps_d[:, (ct % 2) * W:(ct % 2 + 1) * W]
                                    if is_final_q:
                                        if ct >= 2:
                                            v.wait_ge(s_o[ct % 2],
                                                      ob[ct % 2]
                                                      + 16 * ((ct - 2) // 2 + 1))
                                        elif ob[ct % 2]:
                                            v.wait_ge(s_o[ct % 2], ob[ct % 2])
                                        C.inc(v.scalar_tensor_tensor(
                                            t_of[:, (ct % 2) * W:(ct % 2 + 1) * W],
                                            pd, c_ds[:, ct:ct + 1], rsrc(ct),
                                            ALU.mult, ALU.add), s_dve, "s_dve")
                                    else:
                                        C.inc(v.scalar_tensor_tensor(
                                            rsrc(ct), pd, c_ds[:, ct:ct + 1],
                                            rsrc(ct), ALU.mult, ALU.add),
                                            s_dve, "s_dve")
                            d_ev0 = C.get("s_dve") - CT

                            if is_final_q:
                                @blk.sync
                                def _(sy):
                                    for ct in range(CT):
                                        sy.wait_ge(s_dve, d_ev0 + ct + 1)
                                        C.inc(sy.dma_start(
                                            outT[ct * 128:(ct + 1) * 128,
                                                 chunk * W:(chunk + 1) * W],
                                            t_of[:, (ct % 2) * W:(ct % 2 + 1) * W]),
                                            s_o[ct % 2], NO[ct % 2], 16)
                bar()

        def emit_attn_b(is_last=False):
            with sbuf_t("ws2", [128, 2 * HD], BF16) as ws2, \
                 sbuf_t("xn", [128, CT * W], BF16) as xn, \
                 sbuf_t("sig", [128, 8 * TOK], BF16) as sig, \
                 sbuf_t("retr", [128, 8 * TOK], BF16) as retr:
                for chunk in range(2):
                    def rsrc(ct, chunk=chunk):
                        return resid()[:, ct * TOK + chunk * W:
                                     ct * TOK + (chunk + 1) * W]
                    emit_ln(rsrc)
                    emit_ln_apply(rsrc, lambda ct: xn[:, bass.ts(ct, W)])

                    with psum_t("ps_q", [128, 8 * W], F32) as ps_q:
                        wb = [C.get(NA[0]), C.get(NA[1])]
                        pe0 = C.get("s_pe")

                        @blk.gpsimd
                        def _(g):
                            for ct in range(CT):
                                if ct >= 2:
                                    g.wait_ge(s_pe, pe0 + ct - 1)
                                C.inc(g.dma_start(
                                    ws2[:, (ct % 2) * HD:(ct % 2 + 1) * HD],
                                    wq_b[bass.ts(ct, 128), :]),
                                    s_wA[ct % 2], NA[ct % 2], 16)

                        @blk.tensor
                        def _(t):
                            for ct in range(CT):
                                t.wait_ge(s_wA[ct % 2], wb[ct % 2] + 16 * (ct // 2 + 1))
                                slab = ws2[:, (ct % 2) * HD:(ct % 2 + 1) * HD]
                                for ot in range(8):
                                    last = t.matmul(ps_q[:, bass.ts(ot, W)],
                                                    slab[:, bass.ts(ot, 128)],
                                                    xn[:, bass.ts(ct, W)],
                                                    start=(ct == 0),
                                                    stop=(ct == CT - 1))
                                C.inc(last, s_pe, "s_pe")
                        pe_q = C.get("s_pe")
                        q_done = pe_q
                        d_elu0 = C.get("s_dve")
                        a_exp0 = C.get("s_act")

                        @blk.vector
                        def _(v):
                            v.wait_ge(s_pe, pe_q)
                            for ot in range(8):
                                if ot >= 2:
                                    v.wait_ge(s_act, a_exp0 + ot - 1)
                                C.inc(v.tensor_scalar(
                                    t_eA[:, (ot % 2) * W:(ot % 2 + 1) * W],
                                    ps_q[:, bass.ts(ot, W)],
                                    c_bqb[:, ot:ot + 1], 0.0, ALU.add, ALU.min),
                                    s_dve, "s_dve")
                                mx = C.inc(v.tensor_scalar(
                                    t_eC[:, (ot % 2) * W:(ot % 2 + 1) * W],
                                    ps_q[:, bass.ts(ot, W)],
                                    c_bqb[:, ot:ot + 1], 0.0, ALU.add, ALU.max),
                                    s_dve, "s_dve")
                                v.wait_ge(s_act, a_exp0 + ot + 1)
                                v.wait_ge(s_dve, mx)
                                C.inc(v.tensor_add(
                                    sig[:, ot * TOK + chunk * W:
                                        ot * TOK + (chunk + 1) * W],
                                    t_eB[:, (ot % 2) * W:(ot % 2 + 1) * W],
                                    t_eC[:, (ot % 2) * W:(ot % 2 + 1) * W]),
                                    s_dve, "s_dve")

                        @blk.scalar
                        def _(sc):
                            for ot in range(8):
                                sc.wait_ge(s_dve, d_elu0 + 3 * ot + 1)
                                if ot >= 2:
                                    sc.wait_ge(s_dve, d_elu0 + 3 * ot - 3)
                                C.inc(sc.activation(
                                    t_eB[:, (ot % 2) * W:(ot % 2 + 1) * W],
                                    t_eA[:, (ot % 2) * W:(ot % 2 + 1) * W],
                                    AF.Exp), s_act, "s_act")
                        elu_done = C.get("s_dve")

                with psum_t("ps_n", [128, 2 * TOK], F32) as ps_n, \
                     psum_t("ps_dn", [1, TOK], F32) as ps_dn, \
                     psum_t("ps_b", [128, TOK], F32) as ps_b, \
                     sbuf_t("den_r", [1, TOK], F32) as den_r, \
                     sbuf_t("den_rb", [1, TOK], BF16) as den_rb, \
                     sbuf_t("rb_s", [128, TOK], BF16) as rb_s:
                    d_retr = C.get("s_dve")
                    sig_done = elu_done
                    for hd in range(8):
                        @blk.tensor
                        def _(t):
                            if hd == 0:
                                t.wait_ge(s_dve, sig_done)
                            if hd >= 1:
                                t.wait_ge(s_dve, d_retr)
                            for h in range(2):
                                t.matmul(ps_n[:, (hd % 2) * TOK + h * W:
                                              (hd % 2) * TOK + (h + 1) * W],
                                         c_m[:, bass.ts(hd, 128)],
                                         sig[:, hd * TOK + h * W:
                                             hd * TOK + (h + 1) * W],
                                         start=True, stop=True)
                            for h in range(2):
                                last = t.matmul(ps_dn[:, bass.ts(h, W)],
                                                c_z[:, hd:hd + 1],
                                                sig[:, hd * TOK + h * W:
                                                    hd * TOK + (h + 1) * W],
                                                start=True, stop=True)
                            C.inc(last, s_pe, "s_pe")
                        pe_nd = C.get("s_pe")

                        @blk.scalar
                        def _(sc):
                            sc.wait_ge(s_pe, pe_nd)
                            C.inc(sc.activation(den_r[:], ps_dn[:], AF.Copy,
                                                bias=EPS_B), s_act, "s_act")
                        a_den = C.get("s_act")

                        @blk.vector
                        def _(v):
                            v.wait_ge(s_act, a_den)
                            rcp = C.inc(v.reciprocal(den_r[:], den_r[:]),
                                        s_dve, "s_dve")
                            v.wait_ge(s_dve, rcp)
                            C.inc(v.tensor_copy(den_rb[:], den_r[:]), s_dve, "s_dve")
                        d_den = C.get("s_dve")

                        @blk.tensor
                        def _(t):
                            t.wait_ge(s_dve, d_den)
                            for h in range(2):
                                last = t.matmul(ps_b[:, bass.ts(h, W)], c_ones_r[:],
                                                den_rb[:, bass.ts(h, W)],
                                                start=True, stop=True)
                            C.inc(last, s_pe, "s_pe")
                        pe_b = C.get("s_pe")

                        @blk.scalar
                        def _(sc):
                            sc.wait_ge(s_pe, pe_b)
                            C.inc(sc.activation(rb_s[:], ps_b[:], AF.Copy),
                                  s_act, "s_act")
                        a_rb = C.get("s_act")

                        @blk.vector
                        def _(v):
                            v.wait_ge(s_act, a_rb)
                            C.inc(v.tensor_mul(retr[:, bass.ts(hd, TOK)],
                                               ps_n[:, bass.ts(hd % 2, TOK)],
                                               rb_s[:]), s_dve, "s_dve")
                        d_retr = C.get("s_dve")

                with sbuf_t("wfo", [128, 2 * D], F32) as wfo, \
                     sbuf_t("t01", [128, HD], F32) as t01, \
                     sbuf_t("eff", [128, 2 * HD], BF16) as eff, \
                     psum_t("ps_y", [128, 2 * TOK], F32) as ps_y:
                    wb = [C.get(NA[0]), C.get(NA[1])]
                    pe0 = C.get("s_pe")
                    d0 = C.get("s_dve")
                    fold_done = [0] * CT
                    add_done = [0] * CT

                    @blk.vector
                    def _(v):
                        def emit_fold(ct):
                            v.wait_ge(s_wA[ct % 2], wb[ct % 2] + 16 * (ct // 2 + 1))
                            if ct >= 1:
                                v.wait_ge(s_dve, fold_done[ct - 1])
                            if ct >= 2:
                                v.wait_ge(s_pe, pe0 + ct - 1)
                            sl = wfo[:, (ct % 2) * D:(ct % 2 + 1) * D]
                            sl4 = sl.rearrange("p (og j e) -> p og j e", og=8, j=4)
                            ta3 = t01[:].rearrange("p (og e) -> p og e", og=8)
                            eb = eff[:, (ct % 2) * HD:(ct % 2 + 1) * HD]
                            eb3 = eb.rearrange("p (og e) -> p og e", og=8)
                            v.tensor_add(ta3, sl4[:, :, 0, :], sl4[:, :, 1, :])
                            p2 = C.inc(v.tensor_add(eb3, sl4[:, :, 2, :],
                                                    sl4[:, :, 3, :]), s_dve, "s_dve")
                            v.wait_ge(s_dve, p2)
                            fold_done[ct] = C.inc(v.tensor_add(eb3, eb3, ta3),
                                                  s_dve, "s_dve")

                        def emit_add(ct):
                            v.wait_ge(s_pe, pe0 + ct + 1)
                            add_done[ct] = C.inc(v.tensor_add(
                                resid()[:, bass.ts(ct, TOK)],
                                ps_y[:, (ct % 2) * TOK:(ct % 2 + 1) * TOK],
                                resid()[:, bass.ts(ct, TOK)]), s_dve, "s_dve")

                        emit_fold(0)
                        emit_fold(1)
                        for ct in range(2, CT):
                            emit_fold(ct)
                            emit_add(ct - 2)
                        emit_add(CT - 2)
                        emit_add(CT - 1)

                    @blk.gpsimd
                    def _(g):
                        for ct in range(CT):
                            if ct >= 2:
                                g.wait_ge(s_dve, fold_done[ct - 2])
                            else:
                                g.wait_ge(s_pe, pe0)
                                g.wait_ge(s_dve, d0)
                            C.inc(g.dma_start(
                                wfo[:, (ct % 2) * D:(ct % 2 + 1) * D],
                                wo_b[bass.ts(ct, 128), :]),
                                s_wA[ct % 2], NA[ct % 2], 16)

                    @blk.tensor
                    def _(t):
                        for ct in range(CT):
                            t.wait_ge(s_dve, max(fold_done[ct],
                                                 add_done[ct - 2] if ct >= 2 else 0))
                            eb = eff[:, (ct % 2) * HD:(ct % 2 + 1) * HD]
                            psv = ps_y[:, (ct % 2) * TOK:(ct % 2 + 1) * TOK]
                            for ot in range(8):
                                for h in range(2):
                                    last = t.matmul(psv[:, bass.ts(h, W)],
                                                    eb[:, bass.ts(ot, 128)],
                                                    retr[:, ot * TOK + h * W:
                                                         ot * TOK + (h + 1) * W],
                                                    start=(ot == 0), stop=(ot == 7))
                            C.inc(last, s_pe, "s_pe")
                    bar()

        def emit_dump_resid():
            d0 = C.get("s_dve")
            ob = [C.get(NO[0]), C.get(NO[1])]

            @blk.vector
            def _(v):
                for ct in range(CT):
                    for h in range(2):
                        i = 2 * ct + h
                        if i >= 2:
                            v.wait_ge(s_o[i % 2],
                                      ob[i % 2] + 16 * ((i - 2) // 2 + 1))
                        elif ob[i % 2]:
                            v.wait_ge(s_o[i % 2], ob[i % 2])
                        C.inc(v.tensor_copy(
                            t_of[:, (i % 2) * W:(i % 2 + 1) * W],
                            resid()[:, ct * TOK + h * W:ct * TOK + (h + 1) * W]),
                            s_dve, "s_dve")

            @blk.sync
            def _(sy):
                for ct in range(CT):
                    for h in range(2):
                        i = 2 * ct + h
                        sy.wait_ge(s_dve, d0 + i + 1)
                        C.inc(sy.dma_start(
                            outT[ct * 128:(ct + 1) * 128, h * W:(h + 1) * W],
                            t_of[:, (i % 2) * W:(i % 2 + 1) * W]),
                            s_o[i % 2], NO[i % 2], 16)
            bar()

        try:
            if upto >= 1:
                emit_s1()
            if upto >= 2:
                emit_ffn(is_last=(upto == 2))
            if upto >= 3:
                emit_attn_b()
            if upto >= 4:
                emit_ffn(is_last=True)
            if upto in (1, 3):
                emit_dump_resid()
        except _Stop:
            pass

        @blk.sync
        def _(sy):
            sy.wait_ge(s_c, C.get("s_c"))
            sy.wait_ge(s_o[0], C.get(NO[0]))
            sy.wait_ge(s_o[1], C.get(NO[1]))

        @blk.gpsimd
        def _(g):
            g.wait_ge(s_w, C.get("s_w"))
            for se, nm in ((s_wA[0], NA[0]), (s_wA[1], NA[1]),
                           (s_wB[0], NB[0]), (s_wB[1], NB[1]),
                           (s_wD[0], ND[0]), (s_wD[1], ND[1])):
                g.wait_ge(se, C.get(nm))

    return nc


def _host_prep(inputs):
    x = np.asarray(inputs["L0_attn_a_x"])
    XT = np.ascontiguousarray(x.reshape(8192, D).T)

    wqkv = np.asarray(inputs["L0_attn_a_Wqkv"])
    wq_a = np.ascontiguousarray(wqkv[:, 0:HD])
    wk_a = np.ascontiguousarray(wqkv[:, HD:2 * HD])
    wv_a = np.ascontiguousarray(wqkv[:, 2 * HD:3 * HD])
    bqkv = np.asarray(inputs["L0_attn_a_bqkv"])
    bqk_t = np.ascontiguousarray(
        np.concatenate([bqkv[0:HD].reshape(8, 128).T,
                        bqkv[HD:2 * HD].reshape(8, 128).T], axis=1))
    bv_row = np.ascontiguousarray(bqkv[2 * HD:3 * HD].reshape(1, HD))
    wo_a = np.ascontiguousarray(
        np.asarray(inputs["L0_attn_a_Wo"])
        .reshape(DT, 128, CT, 128).transpose(2, 1, 0, 3).reshape(CT * 128, HD))
    bo_t = np.ascontiguousarray(np.asarray(inputs["L0_attn_a_bo"]).reshape(CT, 128).T)

    def i8(w):
        return np.asarray(w).view(np.int8)[..., ::4]

    def gup_tile(wm):
        wt = i8(wm).T
        return np.ascontiguousarray(
            wt.reshape(CT, 128, FT, 128).transpose(2, 1, 0, 3).reshape(FT * 128, D))

    gate_t = gup_tile(inputs["L0_ffn_gate_w"])
    up_t = gup_tile(inputs["L0_ffn_up_w"])
    down_t = np.ascontiguousarray(
        i8(inputs["L0_ffn_down_w"]).T
        .reshape(FT, 128, CT, 128).transpose(2, 1, 0, 3).reshape(CT * 128, DFF))
    gs_t = np.ascontiguousarray(np.asarray(inputs["L0_ffn_gate_s"]).reshape(FT, 128).T)
    us_t = np.ascontiguousarray(np.asarray(inputs["L0_ffn_up_s"]).reshape(FT, 128).T)
    ds_t = np.ascontiguousarray(np.asarray(inputs["L0_ffn_down_s"]).reshape(CT, 128).T)

    wq_sel = np.asarray(inputs["L0_attn_b_Wq"]).reshape(32, 128, D)[::4]
    wq_b = np.ascontiguousarray(wq_sel.reshape(8 * 128, D).T)
    bq_b = np.ascontiguousarray(
        np.asarray(inputs["L0_attn_b_bq"]).reshape(32, 128)[::4].T)
    wo_b = np.ascontiguousarray(
        np.asarray(inputs["L0_attn_b_Wo"]).T
        .reshape(32, 128, CT, 128).transpose(2, 1, 0, 3).reshape(CT * 128, D))
    m_t = np.ascontiguousarray(
        np.asarray(inputs["L0_attn_b_M"]).transpose(1, 0, 2).reshape(128, 8 * 128))
    z_t = np.ascontiguousarray(np.asarray(inputs["L0_attn_b_z"]).T)

    shared = dict(
        wq_a=wq_a, wk_a=wk_a, wv_a=wv_a, bqk_t=bqk_t, bv_row=bv_row,
        wo_a=wo_a, bo_t=bo_t, gate_t=gate_t, up_t=up_t, down_t=down_t,
        gs_t=gs_t, us_t=us_t, ds_t=ds_t, wq_b=wq_b, bq_b=bq_b, wo_b=wo_b,
        m_t=m_t, z_t=z_t,
        cvals=np.tile(np.array([[EPS_LN, EPS_B, SM_SCALE, 1.0 / D]],
                      np.float32), (128, 1)),
        ones_col=np.ones((128, 1), dtype=BF),
        ones_row=np.ones((1, 128), dtype=BF),
        ident=np.eye(128, dtype=BF),
    )
    in_maps = []
    for c in range(8):
        b, own = c // 2, c % 2
        bcols = XT[:, b * SEQ:(b + 1) * SEQ]
        xTc = np.ascontiguousarray(
            np.concatenate([bcols[:, own * TOK:(own + 1) * TOK],
                            bcols[:, (1 - own) * TOK:(2 - own) * TOK]], axis=1))
        m = dict(shared)
        m["xT"] = xTc
        in_maps.append(m)
    return in_maps


def kernel(**inputs):
    in_maps = _host_prep(inputs)
    nc = build_program(upto=4)
    res = run_bass_kernel_spmd(nc, in_maps, core_ids=list(range(8)))
    outT = np.concatenate([np.asarray(res.results[c]["outT"]) for c in range(8)],
                          axis=1)
    return np.ascontiguousarray(outT.T).reshape(4, SEQ, D).astype(np.float32)


# version 06
# speedup vs baseline: 1.1055x; 1.0685x over previous
import contextlib

import numpy as np
import ml_dtypes
import concourse.bass as bass
import concourse.mybir as mybir
from concourse.bass_utils import run_bass_kernel_spmd

F32 = mybir.dt.float32
BF16 = mybir.dt.bfloat16
I8 = mybir.dt.int8
AF = mybir.ActivationFunctionType
ALU = mybir.AluOpType
AX = mybir.AxisListType
BF = ml_dtypes.bfloat16

D = 4096
CT = 32
SEQ = 2048
TOK = 1024
DFF = 11008
FT = 86
W = 512
HD = 1024
DT = 8
EPS_LN = 1e-5
EPS_B = 1e-6
SM_SCALE = 1.0 / 32.0
FQ = [(0, 22), (22, 21), (43, 22), (65, 21)]

S1_GROUPS = 4
S1_SG = 2
S1_OUTPROJ = True
S1_PASSES = ("k", "v", "q")
S1_SGP = ("s", "t", "a")


class Counts:
    def __init__(self):
        self.c = {}

    def inc(self, inst, sem, name, n=1):
        inst.then_inc(sem, n)
        self.c[name] = self.c.get(name, 0) + n
        return self.c[name]

    def get(self, name):
        return self.c.get(name, 0)


class _Stop(Exception):
    pass


def build_program(upto=4, trunc=None):
    nc = bass.Bass("TRN2", target_bir_lowering=False)
    global _last_nc
    _last_nc = nc

    def din(name, shape, dtype):
        return nc.dram_tensor(name, shape, dtype, kind="ExternalInput").ap()

    xT = din("xT", [D, SEQ], F32)
    wq_a = din("wq_a", [D, HD], F32)
    wk_a = din("wk_a", [D, HD], F32)
    wv_a = din("wv_a", [D, HD], F32)
    bqk_t = din("bqk_t", [128, 16], F32)
    bv_row = din("bv_row", [1, HD], F32)
    wo_a = din("wo_a", [CT * 128, HD], F32)
    bo_t = din("bo_t", [128, CT], F32)
    gate_t = din("gate_t", [FT * 128, D], I8)
    up_t = din("up_t", [FT * 128, D], I8)
    down_t = din("down_t", [CT * 128, DFF], I8)
    gs_t = din("gs_t", [128, FT], F32)
    us_t = din("us_t", [128, FT], F32)
    ds_t = din("ds_t", [128, CT], F32)
    wq_b = din("wq_b", [D, HD], F32)
    bq_b = din("bq_b", [128, 8], F32)
    wo_b = din("wo_b", [CT * 128, D], F32)
    m_t = din("m_t", [128, 8 * 128], F32)
    z_t = din("z_t", [128, 8], F32)
    cvals = din("cvals", [128, 4], F32)
    ones_col = din("ones_col", [128, 1], BF16)
    ones_row = din("ones_row", [1, 128], BF16)
    ident = din("ident", [128, 128], BF16)
    outT = nc.dram_tensor("outT", [D, TOK], F32, kind="ExternalOutput").ap()

    C = Counts()

    with contextlib.ExitStack() as st:
        _uid = [0]

        def _mang(name):
            _uid[0] += 1
            return f"{name}_{_uid[0]}"

        def sbuf_t(name, shape, dtype):
            return nc.sbuf_tensor(_mang(name), shape, dtype)

        def psum_t(name, shape, dtype):
            return nc.psum_tensor(_mang(name), shape, dtype)

        def sb(name, shape, dtype):
            return st.enter_context(nc.sbuf_tensor(name, shape, dtype))

        c_bqk = sb("c_bqk", [128, 16], F32)
        c_bo = sb("c_bo", [128, CT], F32)
        c_gs = sb("c_gs", [128, FT], F32)
        c_us = sb("c_us", [128, FT], F32)
        c_ds = sb("c_ds", [128, CT], F32)
        c_bqb = sb("c_bqb", [128, 8], F32)
        c_ones_c = sb("c_ones_c", [128, 1], BF16)
        c_ones_r = sb("c_ones_r", [1, 128], BF16)
        c_id = sb("c_id", [128, 128], BF16)
        c_bv = sb("c_bv", [128, HD], BF16)
        c_m = sb("c_m", [128, 8 * 128], BF16)
        c_z = sb("c_z", [128, 8], BF16)
        sq = sb("sq", [128, 2 * W], BF16)
        ln_rows = sb("ln_rows", [1, 4 * W], F32)
        ln_rbf = sb("ln_rbf", [1, 2 * W], BF16)
        m_b = sb("m_b", [128, W], BF16)
        r_b = sb("r_b", [128, W], BF16)
        m_b2 = sb("m_b2", [128, W], BF16)
        r_b2 = sb("r_b2", [128, W], BF16)
        t_ln = sb("t_ln", [128, 2 * W], BF16)
        t_eA = sb("t_eA", [128, 2 * W], BF16)
        t_eB = sb("t_eB", [128, 2 * W], BF16)
        t_eC = sb("t_eC", [128, 2 * W], BF16)
        t_y = sb("t_y", [128, 2 * TOK], BF16)
        t_of = sb("t_of", [128, 2 * W], F32)
        c_cv = sb("c_cv", [128, 4], F32)
        for _i, _v in enumerate((EPS_LN, EPS_B, SM_SCALE, 1.0 / D)):
            nc.const_aps.register(F32, _v, c_cv[:, _i:_i + 1])
        s_w = st.enter_context(nc.semaphore("s_w"))
        s_wA = [st.enter_context(nc.semaphore("s_wA0")),
                st.enter_context(nc.semaphore("s_wA1"))]
        s_wB = [st.enter_context(nc.semaphore("s_wB0")),
                st.enter_context(nc.semaphore("s_wB1"))]
        s_wD = [st.enter_context(nc.semaphore("s_wD0")),
                st.enter_context(nc.semaphore("s_wD1"))]
        s_o = [st.enter_context(nc.semaphore("s_o0")),
               st.enter_context(nc.semaphore("s_o1"))]
        NA = ("s_wA0", "s_wA1")
        NB = ("s_wB0", "s_wB1")
        ND = ("s_wD0", "s_wD1")
        NO = ("s_o0", "s_o1")
        s_c = st.enter_context(nc.semaphore("s_c"))
        s_pe = st.enter_context(nc.semaphore("s_pe"))
        s_act = st.enter_context(nc.semaphore("s_act"))
        s_dve = st.enter_context(nc.semaphore("s_dve"))
        _blk_holder = [nc.Block().__enter__()]
        st.callback(lambda: _blk_holder[0].__exit__(None, None, None))

        class _BlkProxy:
            def tensor(self, f):
                return _blk_holder[0].tensor(f)

            def scalar(self, f):
                return _blk_holder[0].scalar(f)

            def vector(self, f):
                return _blk_holder[0].vector(f)

            def gpsimd(self, f):
                return _blk_holder[0].gpsimd(f)

            def sync(self, f):
                return _blk_holder[0].sync(f)

        blk = _BlkProxy()
        buf = {"resid": None}

        def resid():
            if buf["resid"] is None:
                buf["resid"] = st.enter_context(
                    nc.sbuf_tensor("resid", [128, CT * TOK], BF16, side="right"))
            return buf["resid"]

        _barn = [0]

        def bar():
            _blk_holder[0].__exit__(None, None, None)
            _blk_holder[0] = nc.Block().__enter__()
            _barn[0] += 1
            if trunc is not None and _barn[0] >= trunc:
                raise _Stop()
            tgt = [(s_pe, C.get("s_pe")), (s_act, C.get("s_act")),
                   (s_dve, C.get("s_dve"))]
            for deco in (blk.tensor, blk.scalar, blk.vector, blk.gpsimd,
                         blk.sync):
                @deco
                def _(e):
                    for se, val in tgt:
                        if val:
                            e.wait_ge(se, val)

        @blk.sync
        def _(sy):
            for dst, src in (
                (c_bqk, bqk_t), (c_bo, bo_t), (c_gs, gs_t), (c_us, us_t),
                (c_ds, ds_t), (c_bqb, bq_b), (c_cv, cvals), (c_ones_c, ones_col),
                (c_ones_r, ones_row), (c_id, ident),
            ):
                C.inc(sy.dma_start(dst[:], src[:]), s_c, "s_c", 16)

        @blk.gpsimd
        def _(g):
            C.inc(g.dma_start(c_m[:], m_t[:]), s_w, "s_w", 16)
            C.inc(g.dma_start(c_z[:], z_t[:]), s_w, "s_w", 16)

        with sbuf_t("bv_r", [1, HD], F32) as bv_r, \
             sbuf_t("bv_rb", [1, HD], BF16) as bv_rb, \
             psum_t("pb0", [128, HD], F32) as pb0:
            @blk.sync
            def _(sy):
                C.inc(sy.dma_start(bv_r[:], bv_row[:]), s_c, "s_c", 16)

            @blk.vector
            def _(v):
                v.wait_ge(s_c, C.get("s_c"))
                C.inc(v.tensor_copy(bv_rb[:], bv_r[:]), s_dve, "s_dve")

            @blk.tensor
            def _(t):
                t.wait_ge(s_dve, C.get("s_dve"))
                t.wait_ge(s_w, C.get("s_w"))
                for h in range(2):
                    last = t.matmul(pb0[:, h * W:(h + 1) * W], c_ones_r[:],
                                    bv_rb[:, h * W:(h + 1) * W], start=True, stop=True)
                C.inc(last, s_pe, "s_pe")

            @blk.scalar
            def _(sc):
                sc.wait_ge(s_pe, C.get("s_pe"))
                C.inc(sc.copy(c_bv[:], pb0[:]), s_act, "s_act")
            for deco in (blk.tensor, blk.scalar, blk.vector, blk.gpsimd,
                         blk.sync):
                @deco
                def _(e):
                    e.wait_ge(s_c, C.get("s_c"))
                    e.wait_ge(s_w, C.get("s_w"))
            bar()

        def emit_ln(src_of_ct, do_bar=True, pe_wait=None, dve_wait=None,
                    mb=None, rb=None):
            mb = m_b if mb is None else mb
            rb = r_b if rb is None else rb
            with psum_t("ps_sum", [1, W], F32) as ps_sum, \
                 psum_t("ps_sq", [1, W], F32) as ps_sq, \
                 psum_t("ps_bc", [128, 2 * W], F32) as ps_bc:
                a0 = C.get("s_act")
                p0 = C.get("s_pe")

                @blk.scalar
                def _(sc):
                    if pe_wait:
                        sc.wait_ge(s_pe, pe_wait)
                    if dve_wait:
                        sc.wait_ge(s_dve, dve_wait)
                    for ct in range(CT):
                        if ct >= 2:
                            sc.wait_ge(s_pe, p0 + ct - 1)
                        C.inc(sc.activation(sq[:, (ct % 2) * W:(ct % 2 + 1) * W],
                                            src_of_ct(ct), AF.Square), s_act, "s_act")

                @blk.tensor
                def _(t):
                    if pe_wait is None and dve_wait:
                        t.wait_ge(s_dve, dve_wait)
                    for ct in range(CT):
                        t.wait_ge(s_act, a0 + ct + 1)
                        t.matmul(ps_sum[:], c_ones_c[:], src_of_ct(ct),
                                 start=(ct == 0), stop=(ct == CT - 1))
                        C.inc(t.matmul(ps_sq[:], c_ones_c[:],
                                       sq[:, (ct % 2) * W:(ct % 2 + 1) * W],
                                       start=(ct == 0), stop=(ct == CT - 1)),
                              s_pe, "s_pe")

                m_row = ln_rows[:, 0:W]
                msq_row = ln_rows[:, W:2 * W]
                var_row = ln_rows[:, 2 * W:3 * W]
                sd_row = ln_rows[:, 3 * W:4 * W]
                m_rb = ln_rbf[:, 0:W]
                rs_rb = ln_rbf[:, W:2 * W]
                pe_stats = C.get("s_pe")

                @blk.vector
                def _(v):
                    v.wait_ge(s_pe, pe_stats)
                    C.inc(v.tensor_scalar_mul(m_row, ps_sum[:], 1.0 / D),
                          s_dve, "s_dve")

                @blk.scalar
                def _(sc):
                    sc.wait_ge(s_pe, pe_stats)
                    C.inc(sc.activation(msq_row, ps_sum[:], AF.Square, scale=1.0 / D),
                          s_act, "s_act")
                a_msq = C.get("s_act")

                @blk.vector
                def _(v):
                    v.wait_ge(s_act, a_msq)
                    C.inc(v.scalar_tensor_tensor(var_row, ps_sq[:], 1.0 / D, msq_row,
                                                 ALU.mult, ALU.subtract),
                          s_dve, "s_dve")
                d_var = C.get("s_dve")

                @blk.scalar
                def _(sc):
                    sc.wait_ge(s_dve, d_var)
                    C.inc(sc.activation(sd_row, var_row, AF.Sqrt, bias=EPS_LN),
                          s_act, "s_act")
                a_sd = C.get("s_act")

                @blk.vector
                def _(v):
                    v.wait_ge(s_act, a_sd)
                    rc = C.inc(v.reciprocal(sd_row, sd_row), s_dve, "s_dve")
                    v.tensor_copy(m_rb, m_row)
                    v.wait_ge(s_dve, rc)
                    C.inc(v.tensor_copy(rs_rb, sd_row), s_dve, "s_dve")
                d_rows = C.get("s_dve")

                @blk.tensor
                def _(t):
                    t.wait_ge(s_dve, d_rows)
                    t.matmul(ps_bc[:, 0:W], c_ones_r[:], m_rb, start=True, stop=True)
                    C.inc(t.matmul(ps_bc[:, W:2 * W], c_ones_r[:], rs_rb,
                                   start=True, stop=True), s_pe, "s_pe")
                pe_bc = C.get("s_pe")

                @blk.scalar
                def _(sc):
                    sc.wait_ge(s_pe, pe_bc)
                    sc.copy(mb[:], ps_bc[:, 0:W])
                    C.inc(sc.copy(rb[:], ps_bc[:, W:2 * W]), s_act, "s_act")
                if do_bar:
                    bar()
                return C.get("s_act")

        def emit_ln_apply(src_of_ct, dst_of_ct, do_bar=True, act_wait=None,
                          pe_wait=None, mb=None, rb=None):
            mb = m_b if mb is None else mb
            rb = r_b if rb is None else rb
            done = [0] * CT

            @blk.vector
            def _(v):
                if act_wait:
                    v.wait_ge(s_act, act_wait)
                if pe_wait:
                    v.wait_ge(s_pe, pe_wait)
                for ct in range(CT):
                    tv = t_ln[:, (ct % 2) * W:(ct % 2 + 1) * W]
                    sc_ = C.inc(v.tensor_sub(tv, src_of_ct(ct), mb[:]),
                                s_dve, "s_dve")
                    v.wait_ge(s_dve, sc_)
                    done[ct] = C.inc(v.tensor_mul(dst_of_ct(ct), tv, rb[:]),
                                     s_dve, "s_dve")
            if do_bar:
                bar()
            return done

        def emit_s1():
            with sbuf_t("ws1", [128, 2 * HD], BF16) as ws1, \
                 sbuf_t("ao_sb", [128, DT * TOK], BF16) as ao_sb, \
                 contextlib.ExitStack() as st_s1:
                k_sb = st_s1.enter_context(sbuf_t("k_sb", [128, DT * SEQ], BF16))
                v_sb = st_s1.enter_context(sbuf_t("v_sb", [128, 16 * HD], BF16))
                q_sb = st_s1.enter_context(sbuf_t("q_sb", [128, DT * TOK], BF16))
                st_xg = contextlib.ExitStack()
                xg = st_xg.enter_context(sbuf_t("xg", [128, CT * W], BF16))
                pass_done = None
                evac_done = None
                for grp in range(S1_GROUPS):
                    @blk.gpsimd
                    def _(g, pd=pass_done):
                        g.wait_ge(s_w, C.get("s_w"))
                        if pd:
                            g.wait_ge(s_pe, pd)
                        C.inc(g.dma_start(
                            xg[:].rearrange("p (ct t) -> p ct t", ct=CT),
                            xT[:, grp * W:(grp + 1) * W]
                            .rearrange("(ct p) t -> p ct t", p=128)),
                            s_w, "s_w", 16)
                    wv_xg = C.get("s_w")
                    for deco in (blk.scalar, blk.tensor, blk.vector):
                        @deco
                        def _(e):
                            e.wait_ge(s_w, wv_xg)
                    a_bc = emit_ln(lambda ct: xg[:, ct * W:(ct + 1) * W],
                                   do_bar=False, pe_wait=pass_done,
                                   dve_wait=evac_done)
                    apply_ct = emit_ln_apply(
                        lambda ct: xg[:, ct * W:(ct + 1) * W],
                        lambda ct: xg[:, ct * W:(ct + 1) * W],
                        do_bar=False, act_wait=a_bc, pe_wait=pass_done)
                    apply_done = apply_ct[-1]

                    def proj_pass(wmat, kind, grp=grp):
                        nonlocal pass_done, evac_done
                        with psum_t("ps_p", [128, 8 * W], F32) as ps_p:
                            wb = [C.get(NA[0]), C.get(NA[1])]
                            pe0 = C.get("s_pe")
                            pd_prev = pass_done
                            ev_prev = evac_done
                            ap_d = apply_done
                            ap_ct = apply_ct
                            first = (wmat is wk_a)

                            @blk.gpsimd
                            def _(g):
                                for ct in range(CT):
                                    if ct >= 2:
                                        g.wait_ge(s_pe, pe0 + ct - 1)
                                    elif pd_prev:
                                        g.wait_ge(s_pe, pd_prev)
                                    C.inc(g.dma_start(
                                        ws1[:, (ct % 2) * HD:(ct % 2 + 1) * HD],
                                        wmat[bass.ts(ct, 128), :]),
                                        s_wA[ct % 2], NA[ct % 2], 16)

                            @blk.tensor
                            def _(t):
                                for ct in range(CT):
                                    t.wait_ge(s_wA[ct % 2],
                                              wb[ct % 2] + 16 * (ct // 2 + 1))
                                    if first:
                                        t.wait_ge(s_dve, ap_ct[ct])
                                    elif ct == 0:
                                        t.wait_ge(s_dve, ap_d)
                                    if ct == 0 and ev_prev:
                                        t.wait_ge(s_dve, ev_prev)
                                    slab = ws1[:, (ct % 2) * HD:(ct % 2 + 1) * HD]
                                    if kind in ("k", "q"):
                                        for dt in range(DT):
                                            last = t.matmul(
                                                ps_p[:, bass.ts(dt, W)],
                                                slab[:, bass.ts(dt, 128)],
                                                xg[:, bass.ts(ct, W)],
                                                start=(ct == 0), stop=(ct == CT - 1))
                                    else:
                                        for ti in range(4):
                                            for h in range(2):
                                                last = t.matmul(
                                                    ps_p[:, ti * HD + h * W:
                                                         ti * HD + (h + 1) * W],
                                                    xg[:, grp * 0 + ct * W + ti * 128:
                                                       ct * W + (ti + 1) * 128],
                                                    slab[:, h * W:(h + 1) * W],
                                                    start=(ct == 0), stop=(ct == CT - 1))
                                    C.inc(last, s_pe, "s_pe")
                            pe_done = C.get("s_pe")

                            if kind == "k":
                                @blk.vector
                                def _(v):
                                    v.wait_ge(s_pe, pe_done)
                                    for dt in range(DT):
                                        C.inc(v.tensor_scalar_add(
                                            k_sb[:, dt * SEQ + grp * W:
                                                 dt * SEQ + (grp + 1) * W],
                                            ps_p[:, bass.ts(dt, W)],
                                            c_bqk[:, 8 + dt:9 + dt]),
                                            s_dve, "s_dve")
                            elif kind == "q":
                                @blk.vector
                                def _(v):
                                    v.wait_ge(s_pe, pe_done)
                                    for dt in range(DT):
                                        C.inc(v.tensor_scalar_add(
                                            q_sb[:, dt * TOK + grp * W:
                                                 dt * TOK + (grp + 1) * W],
                                            ps_p[:, bass.ts(dt, W)],
                                            c_bqk[:, dt:dt + 1]),
                                            s_dve, "s_dve")
                            else:
                                @blk.vector
                                def _(v):
                                    v.wait_ge(s_pe, pe_done)
                                    for ti in range(4):
                                        C.inc(v.tensor_add(
                                            v_sb[:, (grp * 4 + ti) * HD:
                                                 (grp * 4 + ti + 1) * HD],
                                            ps_p[:, bass.ts(ti, HD)], c_bv[:]),
                                            s_dve, "s_dve")
                            pass_done = C.get("s_pe")
                            evac_done = C.get("s_dve")

                    if "k" in S1_PASSES:
                        proj_pass(wk_a, "k")
                    if "v" in S1_PASSES:
                        proj_pass(wv_a, "v")
                    if grp < 2 and "q" in S1_PASSES:
                        proj_pass(wq_a, "q")

                st_xg.close()
                p_sl = st_s1.enter_context(sbuf_t("p_sl", [128, 4 * SEQ], BF16))
                pn_sl = st_s1.enter_context(sbuf_t("pn_sl", [128, 4 * SEQ], BF16))
                pT_sl = st_s1.enter_context(sbuf_t("pT_sl", [128, 16 * W], BF16))
                smx = st_s1.enter_context(sbuf_t("smx", [128, 8], F32))
                if True:
                    sg_gate_act = None
                    for sg in range(S1_SG):
                        with psum_t("ps_s", [128, 2 * SEQ], F32) as ps_s:
                            a_sm0 = C.get("s_act")
                            ev_d = evac_done
                            sg_act = sg_gate_act
                            for tau in range(4):
                                t1 = sg * 4 + tau
                                psv = ps_s[:, (tau % 2) * SEQ:(tau % 2 + 1) * SEQ]

                                @blk.tensor
                                def _(t):
                                    if tau == 0:
                                        t.wait_ge(s_dve, ev_d)
                                        if sg_act:
                                            t.wait_ge(s_act, sg_act)
                                    if tau >= 2:
                                        t.wait_ge(s_act, a_sm0 + 2 * (tau - 2) + 1)
                                    for dt in range(DT):
                                        for b in range(4):
                                            last = t.matmul(
                                                psv[:, bass.ts(b, W)],
                                                q_sb[:, dt * TOK + t1 * 128:
                                                     dt * TOK + (t1 + 1) * 128],
                                                k_sb[:, dt * SEQ + b * W:
                                                     dt * SEQ + (b + 1) * W],
                                                start=(dt == 0), stop=(dt == DT - 1))
                                    C.inc(last, s_pe, "s_pe")
                                pe_sc = C.get("s_pe")

                                @blk.vector
                                def _(v):
                                    v.wait_ge(s_pe, pe_sc)
                                    mc = C.inc(v.reduce_max(smx[:, tau:tau + 1], psv,
                                                            axis=AX.X), s_dve, "s_dve")
                                    v.wait_ge(s_dve, mc)
                                    C.inc(v.tensor_scalar_mul(
                                        smx[:, tau:tau + 1], smx[:, tau:tau + 1],
                                        -SM_SCALE), s_dve, "s_dve")
                                d_mx = C.get("s_dve")

                                @blk.scalar
                                def _(sc):
                                    sc.wait_ge(s_dve, d_mx)
                                    C.inc(sc.activation(
                                        p_sl[:, tau * SEQ:(tau + 1) * SEQ], psv,
                                        AF.Exp, bias=smx[:, tau:tau + 1],
                                        scale=SM_SCALE), s_act, "s_act")
                                a_exp = C.get("s_act")

                                @blk.vector
                                def _(v):
                                    v.wait_ge(s_act, a_exp)
                                    rsc = C.inc(v.reduce_sum(
                                        smx[:, 4 + tau:5 + tau],
                                        p_sl[:, tau * SEQ:(tau + 1) * SEQ],
                                        axis=AX.X), s_dve, "s_dve")
                                    v.wait_ge(s_dve, rsc)
                                    C.inc(v.reciprocal(smx[:, 4 + tau:5 + tau],
                                                       smx[:, 4 + tau:5 + tau]),
                                          s_dve, "s_dve")
                                d_rs = C.get("s_dve")

                                @blk.scalar
                                def _(sc):
                                    sc.wait_ge(s_dve, d_rs)
                                    C.inc(sc.activation(
                                        pn_sl[:, tau * SEQ:(tau + 1) * SEQ],
                                        p_sl[:, tau * SEQ:(tau + 1) * SEQ],
                                        AF.Copy, scale=smx[:, 4 + tau:5 + tau]),
                                        s_act, "s_act")
                        a_pn_done = C.get("s_act")

                        if "t" not in S1_SGP:
                            continue
                        with psum_t("ps_tr", [128, 2 * 1024], BF16) as ps_tr:
                            a_tr0 = C.get("s_act")
                            p_tr0 = C.get("s_pe")

                            @blk.tensor
                            def _(t):
                                for ti in range(16):
                                    if ti == 0:
                                        t.wait_ge(s_act, a_pn_done)
                                    if ti >= 2:
                                        t.wait_ge(s_act, a_tr0 + ti - 1)
                                    for tau in range(4):
                                        last = t.transpose(
                                            ps_tr[:, (ti % 2) * 1024 + tau * 128:
                                                  (ti % 2) * 1024 + (tau + 1) * 128],
                                            pn_sl[:, tau * SEQ + ti * 128:
                                                  tau * SEQ + (ti + 1) * 128],
                                            c_id[:])
                                    C.inc(last, s_pe, "s_pe")

                            @blk.scalar
                            def _(sc):
                                for ti in range(16):
                                    sc.wait_ge(s_pe, p_tr0 + ti + 1)
                                    C.inc(sc.activation(
                                        pT_sl[:, bass.ts(ti, W)],
                                        ps_tr[:, (ti % 2) * 1024:
                                              (ti % 2) * 1024 + W],
                                        AF.Copy), s_act, "s_act")
                        ptr_done = C.get("s_act")

                        if "a" not in S1_SGP:
                            continue
                        with psum_t("ps_av", [128, 2 * W], F32) as ps_av:
                            a_av0 = C.get("s_act")
                            p_av0 = C.get("s_pe")

                            @blk.tensor
                            def _(t):
                                for dt in range(DT):
                                    if dt == 0:
                                        t.wait_ge(s_act, ptr_done)
                                    if dt >= 2:
                                        t.wait_ge(s_act, a_av0 + dt - 1)
                                    for ti in range(16):
                                        last = t.matmul(
                                            ps_av[:, (dt % 2) * W:(dt % 2 + 1) * W],
                                            v_sb[:, ti * HD + dt * 128:
                                                 ti * HD + (dt + 1) * 128],
                                            pT_sl[:, bass.ts(ti, W)],
                                            start=(ti == 0), stop=(ti == 15))
                                    C.inc(last, s_pe, "s_pe")

                            @blk.scalar
                            def _(sc):
                                for dt in range(DT):
                                    sc.wait_ge(s_pe, p_av0 + dt + 1)
                                    C.inc(sc.activation(
                                        ao_sb[:, dt * TOK + sg * W:
                                             dt * TOK + (sg + 1) * W],
                                        ps_av[:, (dt % 2) * W:(dt % 2 + 1) * W],
                                        AF.Copy), s_act, "s_act")
                        sg_gate_act = C.get("s_act")
                        attnv_done = C.get("s_pe")

                st_s1.close()
                if not S1_OUTPROJ:
                    return
                @blk.gpsimd
                def _(g):
                    g.wait_ge(s_w, C.get("s_w"))
                    g.wait_ge(s_pe, attnv_done)
                    g.wait_ge(s_act, sg_gate_act)
                    g.wait_ge(s_dve, C.get("s_dve"))
                    C.inc(g.dma_start(
                        resid()[:].rearrange("p (ct t) -> p ct t", ct=CT),
                        xT[:, 0:TOK].rearrange("(ct p) t -> p ct t", p=128)),
                        s_w, "s_w", 16)
                w_res = C.get("s_w")

                with psum_t("ps_y", [128, 2 * TOK], F32) as ps_y:
                    wb = [C.get(NA[0]), C.get(NA[1])]
                    pe0 = C.get("s_pe")
                    a0 = C.get("s_act")
                    d0 = C.get("s_dve")
                    ao_done = sg_gate_act
                    av_done = attnv_done

                    @blk.gpsimd
                    def _(g):
                        for ct in range(CT):
                            if ct >= 2:
                                g.wait_ge(s_pe, pe0 + ct - 1)
                            elif av_done:
                                g.wait_ge(s_pe, av_done)
                            C.inc(g.dma_start(
                                ws1[:, (ct % 2) * HD:(ct % 2 + 1) * HD],
                                wo_a[bass.ts(ct, 128), :]),
                                s_wA[ct % 2], NA[ct % 2], 16)

                    @blk.tensor
                    def _(t):
                        for ct in range(CT):
                            t.wait_ge(s_wA[ct % 2], wb[ct % 2] + 16 * (ct // 2 + 1))
                            if ct == 0:
                                t.wait_ge(s_act, ao_done)
                            if ct >= 2:
                                t.wait_ge(s_dve, d0 + ct - 1)
                            slab = ws1[:, (ct % 2) * HD:(ct % 2 + 1) * HD]
                            psv = ps_y[:, (ct % 2) * TOK:(ct % 2 + 1) * TOK]
                            for dt in range(DT):
                                for h in range(2):
                                    last = t.matmul(
                                        psv[:, bass.ts(h, W)],
                                        slab[:, bass.ts(dt, 128)],
                                        ao_sb[:, dt * TOK + h * W:
                                             dt * TOK + (h + 1) * W],
                                        start=(dt == 0), stop=(dt == DT - 1))
                            C.inc(last, s_pe, "s_pe")

                    @blk.vector
                    def _(v):
                        v.wait_ge(s_w, w_res)
                        for ct in range(CT):
                            v.wait_ge(s_pe, pe0 + ct + 1)
                            C.inc(v.scalar_tensor_tensor(
                                resid()[:, bass.ts(ct, TOK)],
                                ps_y[:, (ct % 2) * TOK:(ct % 2 + 1) * TOK],
                                c_bo[:, ct:ct + 1],
                                resid()[:, bass.ts(ct, TOK)],
                                ALU.add, ALU.add), s_dve, "s_dve")
                    bar()

        def emit_ffn(is_last):
            with sbuf_t("xn", [128, CT * W], BF16) as xn, \
                 sbuf_t("hbuf", [128, 22 * W], BF16) as hbuf, \
                 sbuf_t("wsA", [128, 2 * D], BF16) as wsA, \
                 sbuf_t("wsB", [128, 2 * D], BF16) as wsB, \
                 sbuf_t("wsD", [128, 2 * 2816], BF16) as wsD:
                gu_done = None
                down_done = None
                h_done_prev = None

                def rs(ct, chunk):
                    return resid()[:, ct * TOK + chunk * W:
                                   ct * TOK + (chunk + 1) * W]
                a_bc1 = emit_ln(lambda ct: rs(ct, 0), do_bar=False)
                a_bc2 = emit_ln(lambda ct: rs(ct, 1), do_bar=False,
                                mb=m_b2, rb=r_b2)
                for chunk in range(2):
                    def rsrc(ct, chunk=chunk):
                        return rs(ct, chunk)
                    apply_ct = emit_ln_apply(
                        rsrc, lambda ct: xn[:, bass.ts(ct, W)], do_bar=False,
                        act_wait=(a_bc1 if chunk == 0 else a_bc2),
                        pe_wait=C.get("s_pe") if chunk else None,
                        mb=(m_b if chunk == 0 else m_b2),
                        rb=(r_b if chunk == 0 else r_b2))
                    apply_done = apply_ct[-1]

                    with psum_t("ps_g", [128, 2 * W], F32) as ps_g, \
                         psum_t("ps_u", [128, 2 * W], F32) as ps_u, \
                         psum_t("ps_d", [128, 2 * W], F32) as ps_d:
                        for qi, (f0, nft) in enumerate(FQ):
                            wba = [C.get(NA[0]), C.get(NA[1])]
                            wbb = [C.get(NB[0]), C.get(NB[1])]
                            pe0 = C.get("s_pe")
                            a0 = C.get("s_act")
                            d0 = C.get("s_dve")
                            gu_prev = gu_done
                            h_prev = h_done_prev
                            ap_ct = apply_ct if qi == 0 else None
                            ln2_act = a_bc2 if (qi == 0 and chunk == 0) else None

                            @blk.gpsimd
                            def _(g):
                                for i in range(nft):
                                    ft = f0 + i
                                    if i >= 2:
                                        g.wait_ge(s_pe, pe0 + i - 1)
                                    elif gu_prev:
                                        g.wait_ge(s_pe, gu_prev)
                                    C.inc(g.dma_start(
                                        wsA[:, (i % 2) * D:(i % 2 + 1) * D],
                                        gate_t[bass.ts(ft, 128), :]),
                                        s_wA[i % 2], NA[i % 2], 16)
                                    C.inc(g.dma_start(
                                        wsB[:, (i % 2) * D:(i % 2 + 1) * D],
                                        up_t[bass.ts(ft, 128), :]),
                                        s_wB[i % 2], NB[i % 2], 16)

                            @blk.tensor
                            def _(t):
                                for i in range(nft):
                                    t.wait_ge(s_wA[i % 2],
                                              wba[i % 2] + 16 * (i // 2 + 1))
                                    t.wait_ge(s_wB[i % 2],
                                              wbb[i % 2] + 16 * (i // 2 + 1))
                                    if i == 0 and ln2_act:
                                        t.wait_ge(s_act, ln2_act)
                                    if i >= 2:
                                        t.wait_ge(s_dve, d0 + i - 1)
                                    elif h_prev:
                                        t.wait_ge(s_dve, h_prev)
                                    gsl = wsA[:, (i % 2) * D:(i % 2 + 1) * D]
                                    usl = wsB[:, (i % 2) * D:(i % 2 + 1) * D]
                                    pg = ps_g[:, (i % 2) * W:(i % 2 + 1) * W]
                                    pu = ps_u[:, (i % 2) * W:(i % 2 + 1) * W]
                                    for ct in range(CT):
                                        if i == 0 and ap_ct:
                                            t.wait_ge(s_dve, ap_ct[ct])
                                        t.matmul(pg, gsl[:, bass.ts(ct, 128)],
                                                 xn[:, bass.ts(ct, W)],
                                                 start=(ct == 0), stop=(ct == CT - 1))
                                        last = t.matmul(pu, usl[:, bass.ts(ct, 128)],
                                                        xn[:, bass.ts(ct, W)],
                                                        start=(ct == 0),
                                                        stop=(ct == CT - 1))
                                    C.inc(last, s_pe, "s_pe")
                            gu_done = C.get("s_pe")

                            @blk.scalar
                            def _(sc):
                                for i in range(nft):
                                    ft = f0 + i
                                    sc.wait_ge(s_pe, pe0 + i + 1)
                                    if i >= 2:
                                        sc.wait_ge(s_dve, d0 + i - 1)
                                    elif h_prev:
                                        sc.wait_ge(s_dve, h_prev)
                                    C.inc(sc.activation(
                                        t_eA[:, (i % 2) * W:(i % 2 + 1) * W],
                                        ps_g[:, (i % 2) * W:(i % 2 + 1) * W],
                                        AF.Silu, scale=c_gs[:, ft:ft + 1]),
                                        s_act, "s_act")

                            @blk.vector
                            def _(v):
                                for i in range(nft):
                                    ft = f0 + i
                                    v.wait_ge(s_act, a0 + i + 1)
                                    C.inc(v.scalar_tensor_tensor(
                                        hbuf[:, bass.ts(i, W)],
                                        ps_u[:, (i % 2) * W:(i % 2 + 1) * W],
                                        c_us[:, ft:ft + 1],
                                        t_eA[:, (i % 2) * W:(i % 2 + 1) * W],
                                        ALU.mult, ALU.mult), s_dve, "s_dve")
                            h_done_prev = C.get("s_dve")
                            h_done = h_done_prev

                            is_final_q = is_last and qi == len(FQ) - 1
                            wbd = [C.get(ND[0]), C.get(ND[1])]
                            pe0d = C.get("s_pe")
                            d0d = C.get("s_dve")
                            ob = [C.get(NO[0]), C.get(NO[1])]
                            dn_prev = down_done

                            @blk.gpsimd
                            def _(g):
                                for ct in range(CT):
                                    if ct >= 2:
                                        g.wait_ge(s_pe, pe0d + ct - 1)
                                    elif dn_prev:
                                        g.wait_ge(s_pe, dn_prev)
                                    C.inc(g.dma_start(
                                        wsD[:, (ct % 2) * 2816:
                                            (ct % 2) * 2816 + nft * 128],
                                        down_t[bass.ts(ct, 128),
                                               f0 * 128:(f0 + nft) * 128]),
                                        s_wD[ct % 2], ND[ct % 2], 16)

                            @blk.tensor
                            def _(t):
                                for ct in range(CT):
                                    t.wait_ge(s_wD[ct % 2],
                                              wbd[ct % 2] + 16 * (ct // 2 + 1))
                                    if ct == 0:
                                        t.wait_ge(s_dve, h_done)
                                    if ct >= 2:
                                        t.wait_ge(s_dve, d0d + ct - 1)
                                    dsl = wsD[:, (ct % 2) * 2816:(ct % 2 + 1) * 2816]
                                    pd = ps_d[:, (ct % 2) * W:(ct % 2 + 1) * W]
                                    for f in range(nft):
                                        last = t.matmul(pd, dsl[:, bass.ts(f, 128)],
                                                        hbuf[:, bass.ts(f, W)],
                                                        start=(f == 0),
                                                        stop=(f == nft - 1))
                                    C.inc(last, s_pe, "s_pe")
                            down_done = C.get("s_pe")

                            @blk.vector
                            def _(v):
                                for ct in range(CT):
                                    v.wait_ge(s_pe, pe0d + ct + 1)
                                    pd = ps_d[:, (ct % 2) * W:(ct % 2 + 1) * W]
                                    if is_final_q:
                                        if ct >= 2:
                                            v.wait_ge(s_o[ct % 2],
                                                      ob[ct % 2]
                                                      + 16 * ((ct - 2) // 2 + 1))
                                        elif ob[ct % 2]:
                                            v.wait_ge(s_o[ct % 2], ob[ct % 2])
                                        C.inc(v.scalar_tensor_tensor(
                                            t_of[:, (ct % 2) * W:(ct % 2 + 1) * W],
                                            pd, c_ds[:, ct:ct + 1], rsrc(ct),
                                            ALU.mult, ALU.add), s_dve, "s_dve")
                                    else:
                                        C.inc(v.scalar_tensor_tensor(
                                            rsrc(ct), pd, c_ds[:, ct:ct + 1],
                                            rsrc(ct), ALU.mult, ALU.add),
                                            s_dve, "s_dve")
                            d_ev0 = C.get("s_dve") - CT

                            if is_final_q:
                                @blk.sync
                                def _(sy):
                                    for ct in range(CT):
                                        sy.wait_ge(s_dve, d_ev0 + ct + 1)
                                        C.inc(sy.dma_start(
                                            outT[ct * 128:(ct + 1) * 128,
                                                 chunk * W:(chunk + 1) * W],
                                            t_of[:, (ct % 2) * W:(ct % 2 + 1) * W]),
                                            s_o[ct % 2], NO[ct % 2], 16)
                bar()

        def emit_attn_b(is_last=False):
            with sbuf_t("ws2", [128, 2 * HD], BF16) as ws2, \
                 sbuf_t("sig", [128, 8 * TOK], BF16) as sig, \
                 sbuf_t("retr", [128, 8 * TOK], BF16) as retr, \
                 contextlib.ExitStack() as st_xn:
                xn = st_xn.enter_context(sbuf_t("xn", [128, CT * W], BF16))
                for chunk in range(2):
                    def rsrc(ct, chunk=chunk):
                        return resid()[:, ct * TOK + chunk * W:
                                     ct * TOK + (chunk + 1) * W]
                    emit_ln(rsrc)
                    emit_ln_apply(rsrc, lambda ct: xn[:, bass.ts(ct, W)])

                    with psum_t("ps_q", [128, 8 * W], F32) as ps_q:
                        wb = [C.get(NA[0]), C.get(NA[1])]
                        pe0 = C.get("s_pe")

                        @blk.gpsimd
                        def _(g):
                            for ct in range(CT):
                                if ct >= 2:
                                    g.wait_ge(s_pe, pe0 + ct - 1)
                                C.inc(g.dma_start(
                                    ws2[:, (ct % 2) * HD:(ct % 2 + 1) * HD],
                                    wq_b[bass.ts(ct, 128), :]),
                                    s_wA[ct % 2], NA[ct % 2], 16)

                        @blk.tensor
                        def _(t):
                            for ct in range(CT):
                                t.wait_ge(s_wA[ct % 2], wb[ct % 2] + 16 * (ct // 2 + 1))
                                slab = ws2[:, (ct % 2) * HD:(ct % 2 + 1) * HD]
                                for ot in range(8):
                                    last = t.matmul(ps_q[:, bass.ts(ot, W)],
                                                    slab[:, bass.ts(ot, 128)],
                                                    xn[:, bass.ts(ct, W)],
                                                    start=(ct == 0),
                                                    stop=(ct == CT - 1))
                                C.inc(last, s_pe, "s_pe")
                        pe_q = C.get("s_pe")
                        q_done = pe_q
                        d_elu0 = C.get("s_dve")
                        a_exp0 = C.get("s_act")

                        @blk.vector
                        def _(v):
                            v.wait_ge(s_pe, pe_q)
                            for ot in range(8):
                                if ot >= 2:
                                    v.wait_ge(s_act, a_exp0 + ot - 1)
                                C.inc(v.tensor_scalar(
                                    t_eA[:, (ot % 2) * W:(ot % 2 + 1) * W],
                                    ps_q[:, bass.ts(ot, W)],
                                    c_bqb[:, ot:ot + 1], 0.0, ALU.add, ALU.min),
                                    s_dve, "s_dve")
                                mx = C.inc(v.tensor_scalar(
                                    t_eC[:, (ot % 2) * W:(ot % 2 + 1) * W],
                                    ps_q[:, bass.ts(ot, W)],
                                    c_bqb[:, ot:ot + 1], 0.0, ALU.add, ALU.max),
                                    s_dve, "s_dve")
                                v.wait_ge(s_act, a_exp0 + ot + 1)
                                v.wait_ge(s_dve, mx)
                                C.inc(v.tensor_add(
                                    sig[:, ot * TOK + chunk * W:
                                        ot * TOK + (chunk + 1) * W],
                                    t_eB[:, (ot % 2) * W:(ot % 2 + 1) * W],
                                    t_eC[:, (ot % 2) * W:(ot % 2 + 1) * W]),
                                    s_dve, "s_dve")

                        @blk.scalar
                        def _(sc):
                            for ot in range(8):
                                sc.wait_ge(s_dve, d_elu0 + 3 * ot + 1)
                                if ot >= 2:
                                    sc.wait_ge(s_dve, d_elu0 + 3 * ot - 3)
                                C.inc(sc.activation(
                                    t_eB[:, (ot % 2) * W:(ot % 2 + 1) * W],
                                    t_eA[:, (ot % 2) * W:(ot % 2 + 1) * W],
                                    AF.Exp), s_act, "s_act")
                        elu_done = C.get("s_dve")

                st_xn.close()
                with psum_t("ps_n", [128, 2 * TOK], F32) as ps_n, \
                     psum_t("ps_dn", [1, TOK], F32) as ps_dn, \
                     psum_t("ps_b", [128, TOK], F32) as ps_b, \
                     sbuf_t("den_r", [1, 2 * TOK], F32) as den_r, \
                     sbuf_t("den_rb", [1, 2 * TOK], BF16) as den_rb, \
                     sbuf_t("rb_s", [128, 2 * TOK], BF16) as rb_s:
                    sig_done = elu_done
                    retr_done = [0] * 8
                    rb_done = [0] * 8
                    aden_done = [0] * 8
                    for hd in range(8):
                        sl = (hd % 2) * TOK

                        @blk.tensor
                        def _(t, sl=sl, hd=hd):
                            if hd == 0:
                                t.wait_ge(s_dve, sig_done)
                            if hd >= 2:
                                t.wait_ge(s_dve, retr_done[hd - 2])
                            for h in range(2):
                                t.matmul(ps_n[:, sl + h * W: sl + (h + 1) * W],
                                         c_m[:, bass.ts(hd, 128)],
                                         sig[:, hd * TOK + h * W:
                                             hd * TOK + (h + 1) * W],
                                         start=True, stop=True)
                            if hd >= 1:
                                t.wait_ge(s_act, aden_done[hd - 1])
                            for h in range(2):
                                last = t.matmul(ps_dn[:, bass.ts(h, W)],
                                                c_z[:, hd:hd + 1],
                                                sig[:, hd * TOK + h * W:
                                                    hd * TOK + (h + 1) * W],
                                                start=True, stop=True)
                            C.inc(last, s_pe, "s_pe")
                        pe_nd = C.get("s_pe")

                        @blk.scalar
                        def _(sc, sl=sl):
                            sc.wait_ge(s_pe, pe_nd)
                            C.inc(sc.activation(den_r[:, sl:sl + TOK],
                                                ps_dn[:], AF.Copy,
                                                bias=EPS_B), s_act, "s_act")
                        aden_done[hd] = C.get("s_act")
                        a_den = aden_done[hd]

                        @blk.vector
                        def _(v, sl=sl):
                            v.wait_ge(s_act, a_den)
                            rcp = C.inc(v.reciprocal(den_r[:, sl:sl + TOK],
                                                     den_r[:, sl:sl + TOK]),
                                        s_dve, "s_dve")
                            v.wait_ge(s_dve, rcp)
                            C.inc(v.tensor_copy(den_rb[:, sl:sl + TOK],
                                                den_r[:, sl:sl + TOK]),
                                  s_dve, "s_dve")
                        d_den = C.get("s_dve")

                        @blk.tensor
                        def _(t, sl=sl, hd=hd):
                            t.wait_ge(s_dve, d_den)
                            if hd >= 1:
                                t.wait_ge(s_act, rb_done[hd - 1])
                            for h in range(2):
                                last = t.matmul(ps_b[:, bass.ts(h, W)],
                                                c_ones_r[:],
                                                den_rb[:, sl + h * W:
                                                        sl + (h + 1) * W],
                                                start=True, stop=True)
                            C.inc(last, s_pe, "s_pe")
                        pe_b = C.get("s_pe")

                        @blk.scalar
                        def _(sc, sl=sl, hd=hd):
                            sc.wait_ge(s_pe, pe_b)
                            C.inc(sc.activation(rb_s[:, sl:sl + TOK],
                                                ps_b[:], AF.Copy),
                                  s_act, "s_act")
                        rb_done[hd] = C.get("s_act")

                        @blk.vector
                        def _(v, sl=sl, hd=hd):
                            v.wait_ge(s_act, rb_done[hd])
                            retr_done[hd] = C.inc(
                                v.tensor_mul(retr[:, bass.ts(hd, TOK)],
                                             ps_n[:, sl:sl + TOK],
                                             rb_s[:, sl:sl + TOK]),
                                s_dve, "s_dve")
                    d_retr = C.get("s_dve")

                with sbuf_t("wfo", [128, 2 * D], F32) as wfo, \
                     sbuf_t("t01", [128, HD], F32) as t01, \
                     sbuf_t("eff", [128, 2 * HD], BF16) as eff, \
                     psum_t("ps_y", [128, 2 * TOK], F32) as ps_y:
                    wb = [C.get(NA[0]), C.get(NA[1])]
                    pe0 = C.get("s_pe")
                    d0 = C.get("s_dve")
                    fold_done = [0] * CT
                    add_done = [0] * CT

                    @blk.vector
                    def _(v):
                        def emit_fold(ct):
                            v.wait_ge(s_wA[ct % 2], wb[ct % 2] + 16 * (ct // 2 + 1))
                            if ct >= 1:
                                v.wait_ge(s_dve, fold_done[ct - 1])
                            if ct >= 2:
                                v.wait_ge(s_pe, pe0 + ct - 1)
                            sl = wfo[:, (ct % 2) * D:(ct % 2 + 1) * D]
                            sl4 = sl.rearrange("p (og j e) -> p og j e", og=8, j=4)
                            ta3 = t01[:].rearrange("p (og e) -> p og e", og=8)
                            eb = eff[:, (ct % 2) * HD:(ct % 2 + 1) * HD]
                            eb3 = eb.rearrange("p (og e) -> p og e", og=8)
                            v.tensor_add(ta3, sl4[:, :, 0, :], sl4[:, :, 1, :])
                            p2 = C.inc(v.tensor_add(eb3, sl4[:, :, 2, :],
                                                    sl4[:, :, 3, :]), s_dve, "s_dve")
                            v.wait_ge(s_dve, p2)
                            fold_done[ct] = C.inc(v.tensor_add(eb3, eb3, ta3),
                                                  s_dve, "s_dve")

                        def emit_add(ct):
                            v.wait_ge(s_pe, pe0 + ct + 1)
                            add_done[ct] = C.inc(v.tensor_add(
                                resid()[:, bass.ts(ct, TOK)],
                                ps_y[:, (ct % 2) * TOK:(ct % 2 + 1) * TOK],
                                resid()[:, bass.ts(ct, TOK)]), s_dve, "s_dve")

                        emit_fold(0)
                        emit_fold(1)
                        for ct in range(2, CT):
                            emit_fold(ct)
                            emit_add(ct - 2)
                        emit_add(CT - 2)
                        emit_add(CT - 1)

                    @blk.gpsimd
                    def _(g):
                        for ct in range(CT):
                            if ct >= 2:
                                g.wait_ge(s_dve, fold_done[ct - 2])
                            else:
                                g.wait_ge(s_pe, pe0)
                                g.wait_ge(s_dve, d0)
                            C.inc(g.dma_start(
                                wfo[:, (ct % 2) * D:(ct % 2 + 1) * D],
                                wo_b[bass.ts(ct, 128), :]),
                                s_wA[ct % 2], NA[ct % 2], 16)

                    @blk.tensor
                    def _(t):
                        for ct in range(CT):
                            t.wait_ge(s_dve, max(fold_done[ct],
                                                 add_done[ct - 2] if ct >= 2 else 0))
                            eb = eff[:, (ct % 2) * HD:(ct % 2 + 1) * HD]
                            psv = ps_y[:, (ct % 2) * TOK:(ct % 2 + 1) * TOK]
                            for ot in range(8):
                                for h in range(2):
                                    last = t.matmul(psv[:, bass.ts(h, W)],
                                                    eb[:, bass.ts(ot, 128)],
                                                    retr[:, ot * TOK + h * W:
                                                         ot * TOK + (h + 1) * W],
                                                    start=(ot == 0), stop=(ot == 7))
                            C.inc(last, s_pe, "s_pe")
                    bar()

        def emit_dump_resid():
            d0 = C.get("s_dve")
            ob = [C.get(NO[0]), C.get(NO[1])]

            @blk.vector
            def _(v):
                for ct in range(CT):
                    for h in range(2):
                        i = 2 * ct + h
                        if i >= 2:
                            v.wait_ge(s_o[i % 2],
                                      ob[i % 2] + 16 * ((i - 2) // 2 + 1))
                        elif ob[i % 2]:
                            v.wait_ge(s_o[i % 2], ob[i % 2])
                        C.inc(v.tensor_copy(
                            t_of[:, (i % 2) * W:(i % 2 + 1) * W],
                            resid()[:, ct * TOK + h * W:ct * TOK + (h + 1) * W]),
                            s_dve, "s_dve")

            @blk.sync
            def _(sy):
                for ct in range(CT):
                    for h in range(2):
                        i = 2 * ct + h
                        sy.wait_ge(s_dve, d0 + i + 1)
                        C.inc(sy.dma_start(
                            outT[ct * 128:(ct + 1) * 128, h * W:(h + 1) * W],
                            t_of[:, (i % 2) * W:(i % 2 + 1) * W]),
                            s_o[i % 2], NO[i % 2], 16)
            bar()

        try:
            if upto >= 1:
                emit_s1()
            if upto >= 2:
                emit_ffn(is_last=(upto == 2))
            if upto >= 3:
                emit_attn_b()
            if upto >= 4:
                emit_ffn(is_last=True)
            if upto in (1, 3):
                emit_dump_resid()
        except _Stop:
            pass

        @blk.sync
        def _(sy):
            sy.wait_ge(s_c, C.get("s_c"))
            sy.wait_ge(s_o[0], C.get(NO[0]))
            sy.wait_ge(s_o[1], C.get(NO[1]))

        @blk.gpsimd
        def _(g):
            g.wait_ge(s_w, C.get("s_w"))
            for se, nm in ((s_wA[0], NA[0]), (s_wA[1], NA[1]),
                           (s_wB[0], NB[0]), (s_wB[1], NB[1]),
                           (s_wD[0], ND[0]), (s_wD[1], ND[1])):
                g.wait_ge(se, C.get(nm))

    return nc


def _host_prep(inputs):
    x = np.asarray(inputs["L0_attn_a_x"])
    XT = np.ascontiguousarray(x.reshape(8192, D).T)

    wqkv = np.asarray(inputs["L0_attn_a_Wqkv"])
    wq_a = np.ascontiguousarray(wqkv[:, 0:HD])
    wk_a = np.ascontiguousarray(wqkv[:, HD:2 * HD])
    wv_a = np.ascontiguousarray(wqkv[:, 2 * HD:3 * HD])
    bqkv = np.asarray(inputs["L0_attn_a_bqkv"])
    bqk_t = np.ascontiguousarray(
        np.concatenate([bqkv[0:HD].reshape(8, 128).T,
                        bqkv[HD:2 * HD].reshape(8, 128).T], axis=1))
    bv_row = np.ascontiguousarray(bqkv[2 * HD:3 * HD].reshape(1, HD))
    wo_a = np.ascontiguousarray(
        np.asarray(inputs["L0_attn_a_Wo"])
        .reshape(DT, 128, CT, 128).transpose(2, 1, 0, 3).reshape(CT * 128, HD))
    bo_t = np.ascontiguousarray(np.asarray(inputs["L0_attn_a_bo"]).reshape(CT, 128).T)

    def i8(w):
        return np.asarray(w).view(np.int8)[..., ::4]

    def gup_tile(wm):
        wt = i8(wm).T
        return np.ascontiguousarray(
            wt.reshape(CT, 128, FT, 128).transpose(2, 1, 0, 3).reshape(FT * 128, D))

    gate_t = gup_tile(inputs["L0_ffn_gate_w"])
    up_t = gup_tile(inputs["L0_ffn_up_w"])
    down_t = np.ascontiguousarray(
        i8(inputs["L0_ffn_down_w"]).T
        .reshape(FT, 128, CT, 128).transpose(2, 1, 0, 3).reshape(CT * 128, DFF))
    gs_t = np.ascontiguousarray(np.asarray(inputs["L0_ffn_gate_s"]).reshape(FT, 128).T)
    us_t = np.ascontiguousarray(np.asarray(inputs["L0_ffn_up_s"]).reshape(FT, 128).T)
    ds_t = np.ascontiguousarray(np.asarray(inputs["L0_ffn_down_s"]).reshape(CT, 128).T)

    wq_sel = np.asarray(inputs["L0_attn_b_Wq"]).reshape(32, 128, D)[::4]
    wq_b = np.ascontiguousarray(wq_sel.reshape(8 * 128, D).T)
    bq_b = np.ascontiguousarray(
        np.asarray(inputs["L0_attn_b_bq"]).reshape(32, 128)[::4].T)
    wo_b = np.ascontiguousarray(
        np.asarray(inputs["L0_attn_b_Wo"]).T
        .reshape(32, 128, CT, 128).transpose(2, 1, 0, 3).reshape(CT * 128, D))
    m_t = np.ascontiguousarray(
        np.asarray(inputs["L0_attn_b_M"]).transpose(1, 0, 2).reshape(128, 8 * 128))
    z_t = np.ascontiguousarray(np.asarray(inputs["L0_attn_b_z"]).T)

    shared = dict(
        wq_a=wq_a, wk_a=wk_a, wv_a=wv_a, bqk_t=bqk_t, bv_row=bv_row,
        wo_a=wo_a, bo_t=bo_t, gate_t=gate_t, up_t=up_t, down_t=down_t,
        gs_t=gs_t, us_t=us_t, ds_t=ds_t, wq_b=wq_b, bq_b=bq_b, wo_b=wo_b,
        m_t=m_t, z_t=z_t,
        cvals=np.tile(np.array([[EPS_LN, EPS_B, SM_SCALE, 1.0 / D]],
                      np.float32), (128, 1)),
        ones_col=np.ones((128, 1), dtype=BF),
        ones_row=np.ones((1, 128), dtype=BF),
        ident=np.eye(128, dtype=BF),
    )
    in_maps = []
    for c in range(8):
        b, own = c // 2, c % 2
        bcols = XT[:, b * SEQ:(b + 1) * SEQ]
        xTc = np.ascontiguousarray(
            np.concatenate([bcols[:, own * TOK:(own + 1) * TOK],
                            bcols[:, (1 - own) * TOK:(2 - own) * TOK]], axis=1))
        m = dict(shared)
        m["xT"] = xTc
        in_maps.append(m)
    return in_maps


def kernel(**inputs):
    in_maps = _host_prep(inputs)
    nc = build_program(upto=4)
    res = run_bass_kernel_spmd(nc, in_maps, core_ids=list(range(8)))
    outT = np.concatenate([np.asarray(res.results[c]["outT"]) for c in range(8)],
                          axis=1)
    return np.ascontiguousarray(outT.T).reshape(4, SEQ, D).astype(np.float32)
